# version 15
# baseline (speedup 1.0000x reference)
import contextlib
import numpy as np
import ml_dtypes
import concourse.bass as bass
import concourse.mybir as mybir
from concourse.bass_utils import run_bass_kernel_spmd

F32 = mybir.dt.float32
BF16 = mybir.dt.bfloat16
AF = mybir.ActivationFunctionType
ALU = mybir.AluOpType
AXX = mybir.AxisListType.X
bf16 = ml_dtypes.bfloat16

D, E, DC, EC = 2048, 8192, 16, 64
TLOC, TT = 1024, 512
H, KVH, DH = 16, 4, 128
NW = float(D * E)
NEGBIG = -1.0e30


class Sched:
    ENGINES = ("sync", "tensor", "vector", "scalar", "gpsimd")

    def __init__(self):
        self.ops = {e: [] for e in self.ENGINES}
        self.counts = {}

    def emit(self, eng, fn, waits=(), inc=None, amt=1):
        tgt = None
        if inc is not None:
            self.counts[inc] = self.counts.get(inc, 0) + amt
            tgt = self.counts[inc]
        ws = tuple((s, v) for (s, v) in waits if v)
        self.ops[eng].append((fn, ws, inc, amt, tgt))
        return tgt

    def n(self, sem):
        return self.counts.get(sem, 0)

    def barrier(self):
        snap = [(s, v) for s, v in self.counts.items() if v > 0]
        for e in self.ENGINES:
            self.ops[e].append((None, tuple(snap), None, 0, None))

    CHAIN = {"vector": "dve", "scalar": "act"}

    def replay(self, name, eng, sems):
        chain_sem = self.CHAIN.get(name)
        prev = 0
        for fn, waits, inc, amt, tgt in self.ops[name]:
            for s, v in waits:
                if chain_sem is not None and s == chain_sem and v <= prev:
                    continue
                eng.wait_ge(sems[s], v)
            if fn is None:
                continue
            if chain_sem is not None and prev > 0:
                eng.wait_ge(sems[chain_sem], prev)
            inst = fn(eng)
            if inc is not None:
                inst.then_inc(sems[inc], amt)
                if inc == chain_sem:
                    prev = tgt


def build_nc(n_cores):
    nc = bass.Bass("TRN2", target_bir_lowering=False)
    sch = Sched()
    emit = sch.emit

    DT = nc.dram_tensor
    x_fm = DT("x_fm", [D, TLOC], F32, kind="ExternalInput")
    wqh = DT("wqh", [H, 128, 2048], F32, kind="ExternalInput")
    wkh = DT("wkh", [KVH, 128, 2048], F32, kind="ExternalInput")
    wvh = DT("wvh", [D, 512], F32, kind="ExternalInput")
    wgh = DT("wgh", [H, 128, 2048], F32, kind="ExternalInput")
    tu_t = DT("tu_t", [EC, 128, 2048], F32, kind="ExternalInput")
    tg_t = DT("tg_t", [EC, 128, 2048], F32, kind="ExternalInput")
    td_t = DT("td_t", [DC, 128, 8192], F32, kind="ExternalInput")
    shard = DT("shard", [3, 128, 16384], F32, kind="ExternalInput")
    cosin = DT("cosin", [2, 64, TLOC], BF16, kind="ExternalInput")
    logpos = DT("logpos", [1, TLOC], BF16, kind="ExternalInput")
    scaler = DT("scaler", [1, H], F32, kind="ExternalInput")
    gnw = DT("gnw", [128, 1], F32, kind="ExternalInput")
    v01 = DT("v01", [128, 1], F32, kind="ExternalInput")
    masks = DT("masks", [2, 128, 128], BF16, kind="ExternalInput")
    identw = DT("identw", [128, 128], BF16, kind="ExternalInput")
    out_fm = DT("out_fm", [D, TLOC], F32, kind="ExternalOutput")
    tub = DT("tub", [EC, 128, 2048], BF16)
    tgb = DT("tgb", [EC, 128, 2048], BF16)
    tdb = DT("tdb", [DC, 128, 8192], BF16)
    sc_in = DT("sc_in", [1, 4], F32)
    sc_out = DT("sc_out", [1, 4], F32)
    sgb = DT("sgb", [H, 128, TLOC], BF16)
    kv_in = DT("kv_in", [8, 128, 1024], BF16)
    kv_out = DT("kv_out", [2, 8, 128, 1024], BF16)

    rg_pair = [[2 * i, 2 * i + 1] for i in range(max(1, n_cores // 2))]
    rg_all = [list(range(n_cores))]

    ctx = contextlib.ExitStack()
    sem_names = ["gdma", "pe", "dve", "act", "cc", "xdma", "sm1", "sm2",
                 "su0", "su1", "sg0", "sg1", "sd0", "sd1", "sw0", "sw1",
                 "cu0", "cu1", "cg0", "cg1", "cd0", "cd1",
                 "sa0", "sa1", "sq2", "ydma", "tdma", "sgw", "sgr0", "sgr1"]
    sems = {s: ctx.enter_context(nc.semaphore(s)) for s in sem_names}

    def sb(name, shape, dt):
        return ctx.enter_context(nc.sbuf_tensor(name, shape, dt))

    def psb(name, shape):
        return ctx.enter_context(nc.psum_tensor(name, shape, F32))

    x = sb("x", [128, DC * TLOC], F32)
    consts = sb("consts", [128, 16], F32)
    onescol = sb("onescol", [128, 1], BF16)
    onescolf = sb("onescolf", [128, 1], F32)
    onesrow = sb("onesrow", [1, 128], BF16)
    onesrowf = sb("onesrowf", [1, 128], F32)
    epsln = sb("epsln", [1, 1], F32)
    epsrms = sb("epsrms", [1, 1], F32)
    gn = sb("gn", [128, 1], F32)
    v01t = sb("v01t", [128, 1], F32)
    scl = sb("scl", [1, H], F32)
    vals = sb("vals", [1, 16], F32)
    totals = sb("totals", [1, 4], F32)
    redt = sb("redt", [128, 4], F32)
    rowa = sb("rowa", [1, TT], F32)
    rowb = sb("rowb", [1, TT], F32)
    rowc = sb("rowc", [1, TT], BF16)
    rowd = sb("rowd", [1, TT], BF16)
    xbt = sb("xbt", [128, TT], BF16)
    xsqt = sb("xsqt", [128, TT], BF16)
    t1t = sb("t1t", [128, TT], F32)
    den_c = sb("den_c", [128, 2], F32)
    den_o = sb("den_o", [128, 2], F32)
    den_t = sb("den_t", [128, 8], F32)
    p0stk = contextlib.ExitStack()
    ps_tot = p0stk.enter_context(nc.psum_tensor("ps_tot", [1, 4], F32))
    ps_bc = p0stk.enter_context(nc.psum_tensor("ps_bc", [128, 16], F32))

    SUP, SGATE, HSD, NTU, NTG, NTD, PTU, PTG, PTD = 0, 1, 2, 3, 4, 5, 6, 7, 8

    last = {}

    def wm(key):
        return list(last.get(key, ()))

    def setwm(key, sem, val):
        last[key] = ((sem, val),)

    def addwm(key, sem, val):
        last[key] = tuple(last.get(key, ())) + ((sem, val),)

    def _init1(v):
        v.memset(totals[:, :], 0.0)
        v.memset(vals[:, :], 0.0)
        v.memset(onescol[:, :], 1.0)
        v.memset(onescolf[:, :], 1.0)
        v.memset(onesrow[:, :], 1.0)
        v.memset(onesrowf[:, :], 1.0)
        v.memset(epsln[:, :], 1e-5)
        return v.memset(epsrms[:, :], 1e-6)
    d_init = emit("vector", _init1, inc="dve")

    for dst_, src_ in ((gn, gnw), (v01t, v01), (scl, scaler)):
        W_SCL = emit("sync", (lambda dst_=dst_, src_=src_: lambda s: s.dma_start(
            out=dst_[:, :], in_=src_.ap()))(), inc="sm1", amt=16)
    emit("vector", lambda v: v.tensor_scalar(
        out=scl[:, :], in0=scl[:, :], scalar1=1.0 / float(np.sqrt(DH)),
        scalar2=None, op0=ALU.mult), waits=[("sm1", W_SCL)], inc="dve")

    for c in range(DC):
        emit("sync", (lambda c=c: lambda s: s.dma_start(
            out=x[:, c * TLOC:(c + 1) * TLOC],
            in_=x_fm.ap()[c * 128:(c + 1) * 128, :]))(), inc="xdma", amt=16)
    W_X = sch.n("xdma")

    with nc.sbuf_tensor("shard_buf", [128, 16384], F32) as shard_buf:
        dprev = d_init
        for m in range(3):
            wg_ = emit("gpsimd", (lambda m=m: lambda g: g.dma_start(
                out=shard_buf[:, :], in_=shard.ap()[m]))(),
                waits=[("dve", dprev)], inc="gdma", amt=16)
            dprev = emit("vector", (lambda m=m: lambda v: v.tensor_reduce(
                redt[:, m:m + 1], shard_buf[:, :], AXX, ALU.add,
                apply_absolute_value=True))(),
                waits=[("gdma", wg_)], inc="dve")
    p_tot = emit("tensor", lambda t: t.matmul(
        ps_tot[:, 0:3], onescolf[:, :], redt[:, 0:3], start=True, stop=True),
        waits=[("dve", dprev)], inc="pe")
    d_tot = emit("vector", lambda v: v.tensor_copy(totals[:, 0:3], ps_tot[:, 0:3]),
                 waits=[("pe", p_tot)], inc="dve")
    wg_ = emit("gpsimd", lambda g: g.dma_start(out=sc_in.ap(), in_=totals[:, 0:4]),
               waits=[("dve", d_tot)], inc="gdma", amt=16)
    c_ar = emit("gpsimd", lambda g: g.collective_compute(
        "AllReduce", ALU.add, replica_groups=rg_all,
        ins=[sc_in.ap()], outs=[sc_out.ap()]),
        waits=[("gdma", wg_)], inc="cc")
    wg_ = emit("gpsimd", lambda g: g.dma_start(out=totals[:, 0:4], in_=sc_out.ap()),
               waits=[("cc", c_ar)], inc="gdma", amt=16)

    d_v = emit("vector", lambda v: v.tensor_scalar(
        out=rowa[:, 0:3], in0=totals[:, 0:3],
        scalar1=1.0 / NW, scalar2=1e-8, op0=ALU.mult, op1=ALU.add),
        waits=[("gdma", wg_)], inc="dve")
    d_v = emit("vector", lambda v: v.tensor_scalar(
        out=vals[:, SUP:SUP + 2], in0=rowa[:, 0:2],
        scalar1=0.5, scalar2=None, op0=ALU.mult),
        waits=[("dve", d_v)], inc="dve")
    d_v = emit("vector", lambda v: v.tensor_scalar(
        out=vals[:, HSD:HSD + 1], in0=rowa[:, 2:3],
        scalar1=0.25, scalar2=None, op0=ALU.mult),
        waits=[("dve", d_v)], inc="dve")
    d_v = emit("vector", lambda v: v.tensor_scalar(
        out=vals[:, NTU:NTU + 3], in0=rowa[:, 0:3],
        scalar1=-0.5, scalar2=None, op0=ALU.mult),
        waits=[("dve", d_v)], inc="dve")
    d_v = emit("vector", lambda v: v.tensor_scalar(
        out=vals[:, PTU:PTU + 3], in0=rowa[:, 0:3],
        scalar1=0.5, scalar2=None, op0=ALU.mult),
        waits=[("dve", d_v)], inc="dve")
    p_bc = emit("tensor", lambda t: t.matmul(
        ps_bc[:, :], onesrowf[:, :], vals[:, :], start=True, stop=True),
        waits=[("dve", d_v)], inc="pe")
    D_CONSTS = emit("vector", lambda v: v.tensor_copy(consts[:, :], ps_bc[:, :]),
                    waits=[("pe", p_bc)], inc="dve")
    p0stk.close()

    sch.barrier()

    def layernorm(t0, dst, ps_r1, ps_r2, ps_v1, ps_v2, pre=()):
        d2 = pe = 0
        for c in range(DC):
            xs = x[:, c * TLOC + t0: c * TLOC + t0 + TT]
            d1 = emit("vector", (lambda xs=xs: lambda v: v.tensor_copy(xbt[:, :], xs))(),
                      waits=wm("xbt") + ([("xdma", W_X)] if t0 == 0 else []), inc="dve")
            d2 = emit("vector", lambda v: v.tensor_mul(xsqt[:, :], xbt[:, :], xbt[:, :]),
                      waits=[("dve", d1)] + wm("xsqt"), inc="dve")
            w0 = [("dve", d2)] + (wm("ln_rows") + list(pre) if c == 0 else [])
            emit("tensor", (lambda c=c: lambda t: t.matmul(
                ps_r1, onescol[:, :], xbt[:, :],
                start=(c == 0), stop=(c == DC - 1)))(), waits=w0, inc="pe")
            pe = emit("tensor", (lambda c=c: lambda t: t.matmul(
                ps_r2, onescol[:, :], xsqt[:, :],
                start=(c == 0), stop=(c == DC - 1)))(), inc="pe")
            setwm("xbt", "pe", pe)
            setwm("xsqt", "pe", pe)
        d = emit("vector", lambda v: v.tensor_scalar(
            out=rowa[:, :], in0=ps_r1, scalar1=1.0 / D, scalar2=None, op0=ALU.mult),
            waits=[("pe", pe)] + wm("rowa"), inc="dve")
        d = emit("vector", lambda v: v.tensor_mul(rowb[:, :], rowa[:, :], rowa[:, :]),
                 waits=[("dve", d)], inc="dve")
        d = emit("vector", lambda v: v.scalar_tensor_tensor(
            rowb[:, :], ps_r2, 1.0 / D, rowb[:, :], ALU.mult, ALU.subtract),
            waits=[("dve", d)], inc="dve")
        setwm("ln_rows", "dve", d)
        a0 = emit("scalar", lambda s: s.activation(
            out=rowb[:, :], in_=rowb[:, :], func=AF.Ln,
            bias=epsln[:, :], scale=1.0), waits=[("dve", d)], inc="act")
        a = emit("scalar", lambda s: s.activation(
            out=rowc[:, :], in_=rowb[:, :], func=AF.Exp, scale=-0.5),
            waits=[("act", a0)] + wm("rowc"), inc="act")
        d = emit("vector", lambda v: v.tensor_mul(rowd[:, :], rowa[:, :], rowc[:, :]),
                 waits=[("act", a)] + wm("rowd"), inc="dve")
        setwm("rowa", "dve", d)
        emit("tensor", lambda t: t.matmul(
            ps_v1, onesrow[:, :], rowc[:, :], start=True, stop=True),
            waits=[("dve", d)] + wm("ln_bc") + list(pre), inc="pe")
        pe = emit("tensor", lambda t: t.matmul(
            ps_v2, onesrow[:, :], rowd[:, :], start=True, stop=True), inc="pe")
        setwm("rowc", "pe", pe)
        setwm("rowd", "pe", pe)
        dl = 0
        for c in range(DC):
            xs = x[:, c * TLOC + t0: c * TLOC + t0 + TT]
            d1 = emit("vector", (lambda xs=xs: lambda v: v.tensor_mul(
                t1t[:, :], xs, ps_v1))(), waits=[("pe", pe), ("dve", dl)], inc="dve")
            dl = emit("vector", (lambda c=c: lambda v: v.tensor_tensor(
                dst(c), t1t[:, :], ps_v2, ALU.subtract))(),
                waits=[("dve", d1)], inc="dve")
        setwm("ln_bc", "dve", dl)
        return dl

    qctr = [0]

    def quant_half(ft, dst_ap, dst_key, ntau_idx, src_hbm_ap, act_path=None):
        qp = qctr[0] % 4
        qs = qp % 2
        if act_path is None:
            act_path = qp < 2
        qctr[0] += 1
        sw = f"sw{qs}"
        wv_ = emit("sync", (lambda src_hbm_ap=src_hbm_ap, qp=qp: lambda s: s.dma_start(
            out=ft["wfp"][qp][:, :], in_=src_hbm_ap))(),
            waits=wm(("wfp", qp)), inc=sw, amt=16)
        ntau = consts[:, ntau_idx:ntau_idx + 1]
        ptau = consts[:, ntau_idx + 3:ntau_idx + 4]
        if act_path:
            a = emit("scalar", (lambda qp=qp, qs=qs, ntau=ntau: lambda s: s.activation(
                out=ft["qa"][qs][:, :], in_=ft["wfp"][qp][:, :], func=AF.Sign,
                bias=ntau, scale=1.0))(),
                waits=[(sw, wv_), ("dve", D_CONSTS)] + wm(("qa", qs)), inc="act")
            c = emit("scalar", (lambda qp=qp, qs=qs, ptau=ptau: lambda s: s.activation(
                out=ft["qb"][qs][:, :], in_=ft["wfp"][qp][:, :], func=AF.Sign,
                bias=ptau, scale=1.0))(),
                waits=wm(("qb", qs)), inc="act")
            d3 = emit("vector", (lambda dst_ap=dst_ap, qs=qs: lambda v: v.tensor_tensor(
                dst_ap, ft["qa"][qs][:, :], ft["qb"][qs][:, :], ALU.add))(),
                waits=[("act", c)] + wm(dst_key), inc="dve")
            setwm(("qa", qs), "dve", d3)
            setwm(("qb", qs), "dve", d3)
            last[("wfp", qp)] = (("act", c),)
        else:
            d1 = emit("vector", (lambda qp=qp, qs=qs, ptau=ptau: lambda v: v.tensor_scalar(
                out=ft["qa"][qs][:, :], in0=ft["wfp"][qp][:, :], scalar1=ptau,
                scalar2=2.0, op0=ALU.is_gt, op1=ALU.mult))(),
                waits=[(sw, wv_), ("dve", D_CONSTS)] + wm(("qa", qs)), inc="dve")
            d2 = emit("vector", (lambda qp=qp, qs=qs, ntau=ntau: lambda v: v.tensor_scalar(
                out=ft["qb"][qs][:, :], in0=ft["wfp"][qp][:, :], scalar1=ntau,
                scalar2=2.0, op0=ALU.is_lt, op1=ALU.mult))(),
                waits=wm(("qb", qs)), inc="dve")
            d3 = emit("vector", (lambda dst_ap=dst_ap, qs=qs: lambda v: v.tensor_tensor(
                dst_ap, ft["qa"][qs][:, :], ft["qb"][qs][:, :], ALU.subtract))(),
                waits=[("dve", d2)] + wm(dst_key), inc="dve")
            setwm(("qa", qs), "dve", d3)
            setwm(("qb", qs), "dve", d3)
            last[("wfp", qp)] = (("dve", d2),)
        return d3

    def stream_col(ft, kind, src_fp, cache, idx, dstbuf, par, ntau_idx, quant,
                   cache_slice=None):
        key = ("col", kind, par)
        if quant:
            mmw = []
            for h2 in range(2):
                d3 = quant_half(
                    ft, dstbuf[par][:, h2 * 1024:(h2 + 1) * 1024], key, ntau_idx,
                    src_fp[:, h2 * 1024:(h2 + 1) * 1024],
                    act_path=(kind == "u"))
                mmw.append(("dve", d3))
            cs = f"c{kind}{par}"
            cw = emit("sync", (lambda cache_slice=cache_slice, dstbuf=dstbuf, par=par:
                               lambda s: s.dma_start(
                                   out=cache_slice, in_=dstbuf[par][:, :]))(),
                      waits=[mmw[-1]], inc=cs, amt=16)
            last[("cache", kind, par)] = ((cs, cw),)
            return mmw, key
        else:
            cs = f"s{kind}{par}"
            wv_ = emit("sync", (lambda cache_slice=cache_slice, dstbuf=dstbuf, par=par:
                                lambda s: s.dma_start(
                                    out=dstbuf[par][:, :], in_=cache_slice))(),
                       waits=wm(key), inc=cs, amt=16)
            return [(cs, wv_)], key

    def ffn_sweep(ft, sw, quant):
        t0 = sw * TT
        pre = (wm(("ps_up", 0)) + wm(("ps_up", 1)) +
               wm(("ps_gt", 0)) + wm(("ps_gt", 1)))
        d_ln = layernorm(t0, lambda c: ft["xln"][:, c * TT:(c + 1) * TT],
                         ft["ps_up"][0][0:1, :], ft["ps_gt"][0][0:1, :],
                         ft["ps_up"][1][:, :], ft["ps_gt"][1][:, :], pre=pre)
        for i in range(2):
            setwm(("ps_up", i), "dve", d_ln)
            setwm(("ps_gt", i), "dve", d_ln)

        d_hid = 0
        for ec in range(EC):
            par = ec % 2
            mmw_g, key_g = stream_col(ft, "g", tg_t.ap()[ec], tgb, ec, ft["gcol"],
                                      par, NTG, quant, tgb.ap()[ec])
            mmw_u, key_u = stream_col(ft, "u", tu_t.ap()[ec], tub, ec, ft["ucol"],
                                      par, NTU, quant, tub.ap()[ec])
            for dc in range(DC):
                w = []
                if dc == 0:
                    w = [mmw_u[0]] + wm(("ps_up", par)) + [("dve", d_ln)]
                elif dc == 8:
                    w = [mmw_u[-1]]
                emit("tensor", (lambda dc=dc, par=par: lambda t: t.matmul(
                    ft["ps_up"][par][:, :], ft["ucol"][par][:, dc * 128:(dc + 1) * 128],
                    ft["xln"][:, dc * TT:(dc + 1) * TT],
                    start=(dc == 0), stop=(dc == DC - 1)))(), waits=w, inc="pe")
            pe_u = sch.n("pe")
            for dc in range(DC):
                w = []
                if dc == 0:
                    w = [mmw_g[0]] + wm(("ps_gt", par))
                elif dc == 8:
                    w = [mmw_g[-1]]
                emit("tensor", (lambda dc=dc, par=par: lambda t: t.matmul(
                    ft["ps_gt"][par][:, :], ft["gcol"][par][:, dc * 128:(dc + 1) * 128],
                    ft["xln"][:, dc * TT:(dc + 1) * TT],
                    start=(dc == 0), stop=(dc == DC - 1)))(), waits=w, inc="pe")
            pe_g = sch.n("pe")
            last[key_u] = (("pe", pe_u),) + last.get(("cache", "u", par), ())
            last[key_g] = (("pe", pe_g),) + last.get(("cache", "g", par), ())
            a1 = emit("scalar", (lambda par=par: lambda s: s.activation(
                out=ft["hsil"][:, :], in_=ft["ps_up"][par][:, :], func=AF.Silu,
                scale=consts[:, SUP:SUP + 1]))(),
                waits=[("pe", pe_u)] + wm("hsil"), inc="act")
            a2 = emit("scalar", (lambda par=par, ec=ec: lambda s: s.activation(
                out=ft["hid"][:, ec * TT:(ec + 1) * TT], in_=ft["ps_gt"][par][:, :],
                func=AF.Copy, scale=consts[:, SGATE:SGATE + 1]))(),
                waits=[("pe", pe_g)], inc="act")
            setwm(("ps_up", par), "act", a1)
            setwm(("ps_gt", par), "act", a2)
            d_hid = emit("vector", (lambda ec=ec: lambda v: v.tensor_mul(
                ft["hid"][:, ec * TT:(ec + 1) * TT], ft["hsil"][:, :],
                ft["hid"][:, ec * TT:(ec + 1) * TT]))(),
                waits=[("act", a2), ("act", a1)], inc="dve")
            setwm("hsil", "dve", d_hid)

        for mc in range(DC):
            par = mc % 2
            pe_d = 0
            for q in range(4):
                qp = (4 * mc + q) % 2
                key_d = ("col", "d", qp)
                if quant:
                    mmw = []
                    for h2 in range(2):
                        off = (2 * q + h2) * 1024
                        d3 = quant_half(ft, ft["dcol"][qp][:, h2 * 1024:(h2 + 1) * 1024],
                                        key_d, NTD, td_t.ap()[mc][:, off:off + 1024])
                        mmw.append(("dve", d3))
                    cw = emit("sync", (lambda mc=mc, q=q, qp=qp: lambda s: s.dma_start(
                        out=tdb.ap()[mc][:, q * 2048:(q + 1) * 2048],
                        in_=ft["dcol"][qp][:, :]))(), waits=[mmw[-1]],
                        inc=f"cd{qp}", amt=16)
                    cextra = ((f"cd{qp}", cw),)
                else:
                    wv_ = emit("sync", (lambda mc=mc, q=q, qp=qp: lambda s: s.dma_start(
                        out=ft["dcol"][qp][:, :],
                        in_=tdb.ap()[mc][:, q * 2048:(q + 1) * 2048]))(),
                        waits=wm(key_d), inc=f"sd{qp}", amt=16)
                    mmw = [(f"sd{qp}", wv_)]
                    cextra = ()
                for j in range(16):
                    ec = q * 16 + j
                    w = []
                    if j == 0:
                        w = [mmw[0]]
                        if q == 0:
                            w += [("dve", d_hid)] + wm(("ps_dn", par))
                    elif j == 8:
                        w = [mmw[-1]]
                    emit("tensor", (lambda j=j, ec=ec, par=par, qp=qp: lambda t: t.matmul(
                        ft["ps_dn"][par][:, :], ft["dcol"][qp][:, j * 128:(j + 1) * 128],
                        ft["hid"][:, ec * TT:(ec + 1) * TT],
                        start=(ec == 0), stop=(ec == EC - 1)))(), waits=w, inc="pe")
                pe_d = sch.n("pe")
                last[key_d] = (("pe", pe_d),) + cextra
            dr = emit("vector", (lambda mc=mc, par=par, t0=t0: lambda v: v.scalar_tensor_tensor(
                x[:, mc * TLOC + t0: mc * TLOC + t0 + TT],
                ft["ps_dn"][par][:, :], consts[:, HSD:HSD + 1],
                x[:, mc * TLOC + t0: mc * TLOC + t0 + TT],
                ALU.mult, ALU.add))(), waits=[("pe", pe_d)], inc="dve")
            setwm(("ps_dn", par), "dve", dr)
            setwm(("xout", t0, mc), "dve", dr)

    def ffn_alloc(stk, pfx, with_quant):
        t = {}
        t["xln"] = stk.enter_context(nc.sbuf_tensor(pfx + "xln", [128, DC * TT], BF16))
        t["hid"] = stk.enter_context(nc.sbuf_tensor(pfx + "hid", [128, EC * TT], BF16))
        t["hsil"] = stk.enter_context(nc.sbuf_tensor(pfx + "hsl", [128, TT], BF16))
        t["ucol"] = [stk.enter_context(nc.sbuf_tensor(pfx + f"uc{i}", [128, 2048], BF16)) for i in range(2)]
        t["gcol"] = [stk.enter_context(nc.sbuf_tensor(pfx + f"gc{i}", [128, 2048], BF16)) for i in range(2)]
        t["dcol"] = [stk.enter_context(nc.sbuf_tensor(pfx + f"dc{i}", [128, 2048], BF16)) for i in range(2)]
        if with_quant:
            t["wfp"] = [stk.enter_context(nc.sbuf_tensor(pfx + f"wf{i}", [128, 1024], F32)) for i in range(4)]
            t["qa"] = [stk.enter_context(nc.sbuf_tensor(pfx + f"qa{i}", [128, 1024], BF16)) for i in range(2)]
            t["qb"] = [stk.enter_context(nc.sbuf_tensor(pfx + f"qb{i}", [128, 1024], BF16)) for i in range(2)]
        t["ps_up"] = [stk.enter_context(nc.psum_tensor(pfx + f"pu{i}", [128, TT], F32)) for i in range(2)]
        t["ps_gt"] = [stk.enter_context(nc.psum_tensor(pfx + f"pg{i}", [128, TT], F32)) for i in range(2)]
        t["ps_dn"] = [stk.enter_context(nc.psum_tensor(pfx + f"pd{i}", [128, TT], F32)) for i in range(2)]
        return t

    def clear_ffn_keys():
        for key in list(last.keys()):
            if isinstance(key, tuple) and key and key[0] in (
                    "ps_up", "ps_gt", "ps_dn", "col", "cache", "wfp", "qa"):
                del last[key]

    with contextlib.ExitStack() as fstk:
        ft = ffn_alloc(fstk, "f1", True)
        ffn_sweep(ft, 0, quant=True)
        ffn_sweep(ft, 1, quant=False)
    sch.barrier()
    clear_ffn_keys()

    astk = contextlib.ExitStack()
    s2 = astk.enter_context
    xlnf = s2(nc.sbuf_tensor("a_xln", [128, DC * TLOC], BF16))
    cos_t = s2(nc.sbuf_tensor("a_cos", [64, TLOC], BF16))
    sin_t = s2(nc.sbuf_tensor("a_sin", [64, TLOC], BF16))
    logp = s2(nc.sbuf_tensor("a_logp", [1, TLOC], BF16))
    mwin = s2(nc.sbuf_tensor("a_mwin", [128, 128], BF16))
    mdiag = s2(nc.sbuf_tensor("a_mdg", [128, 128], BF16))
    ident = s2(nc.sbuf_tensor("a_id", [128, 128], BF16))
    K_all = s2(nc.sbuf_tensor("a_K", [128, KVH * 2048], BF16))
    V_all = s2(nc.sbuf_tensor("a_V", [128, 16 * 512], BF16))
    wv_sb = s2(nc.sbuf_tensor("a_wv", [128, DC * 512], BF16))
    afp = [s2(nc.sbuf_tensor(f"a_fp{i}", [128, 512], F32)) for i in range(2)]
    acol = [s2(nc.sbuf_tensor(f"a_cl{i}", [128, 2048], BF16)) for i in range(2)]
    praw = s2(nc.sbuf_tensor("a_praw", [128, TT], F32))
    sgtmp = s2(nc.sbuf_tensor("a_sgt", [128, TT], BF16))
    q2s = s2(nc.sbuf_tensor("a_q2s", [64, TT], F32))
    sc1 = s2(nc.sbuf_tensor("a_sc1", [64, TT], F32))
    sc2 = s2(nc.sbuf_tensor("a_sc2", [64, TT], F32))
    y2s = s2(nc.sbuf_tensor("a_y2s", [64, TT], BF16))
    psq = s2(nc.sbuf_tensor("a_psq", [128, TT], BF16))
    q_s = s2(nc.sbuf_tensor("a_qs", [128, TLOC], BF16))
    sg_h = s2(nc.sbuf_tensor("a_sg", [128, 2 * TLOC], BF16))
    p_sb = s2(nc.sbuf_tensor("a_p", [128, 3 * 1152], BF16))
    pT = s2(nc.sbuf_tensor("a_pT", [128, 1152], BF16))
    z_t = s2(nc.sbuf_tensor("a_z", [128, 384], F32))
    zq_t = s2(nc.sbuf_tensor("a_zq", [128, 384], BF16))
    rr1 = s2(nc.sbuf_tensor("a_r1", [1, TT], F32))
    rr2 = s2(nc.sbuf_tensor("a_r2", [1, TT], F32))
    pp = [s2(nc.psum_tensor("a_pp0", [128, TT], F32))]
    ptp = s2(nc.psum_tensor("a_ptp", [128, 1024], BF16))
    strip = [s2(nc.psum_tensor(f"a_st{i}", [128, 1536], F32)) for i in range(2)]
    ps_y = pp[0][:, 0:128]
    ps_bz = pp[0][:, 128:256]
    ps_zr = pp[0][0:1, 0:TT]
    ps_zz = pp[0][0:1, 256:384]

    for dst_, src_ in ((cos_t, cosin.ap()[0]), (sin_t, cosin.ap()[1]),
                       (logp, logpos.ap()), (mwin, masks.ap()[0]),
                       (mdiag, masks.ap()[1]), (ident, identw.ap())):
        W_SM2 = emit("sync", (lambda dst_=dst_, src_=src_: lambda s: s.dma_start(
            out=dst_[:, :], in_=src_))(), inc="sm2", amt=16)

    for sw2 in range(2):
        pre = wm(("pp", 0)) + wm(("strip", 0)) + wm(("strip", 1))
        layernorm(sw2 * TT,
                  (lambda sw2=sw2: lambda c: xlnf[:, c * TLOC + sw2 * TT: c * TLOC + sw2 * TT + TT])(),
                  strip[0][0:1, 512:1024], strip[1][0:1, 512:1024],
                  strip[0][:, 0:TT], strip[1][:, 0:TT], pre=pre)
        d_ln2 = sch.n("dve")
        setwm(("pp", 0), "dve", d_ln2)
        for i in range(2):
            setwm(("strip", i), "dve", d_ln2)

    def proj_col(w_hbm, col_i, par):
        d = 0
        for h2 in range(4):
            qp = h2 % 2
            wv_ = emit("sync", (lambda w_hbm=w_hbm, col_i=col_i, h2=h2, qp=qp: lambda s: s.dma_start(
                out=afp[qp][:, :], in_=w_hbm.ap()[col_i][:, h2 * 512:(h2 + 1) * 512]))(),
                waits=wm(("afp", qp)), inc=f"sa{qp}", amt=16)
            d = emit("vector", (lambda par=par, h2=h2, qp=qp: lambda v: v.tensor_copy(
                acol[par][:, h2 * 512:(h2 + 1) * 512], afp[qp][:, :]))(),
                waits=[(f"sa{qp}", wv_)] + (wm(("acol", par)) if h2 == 0 else []), inc="dve")
            setwm(("afp", qp), "dve", d)
        return d

    def proj_mm_half(par, ppar, half, extra=(), ps=None, pskey=("pp", 0)):
        ps_ap = pp[0][:, :] if ps is None else ps
        for dc in range(DC):
            w = []
            if dc == 0:
                w = list(extra) + wm(pskey)
            emit("tensor", (lambda dc=dc, par=par, half=half, ps_ap=ps_ap: lambda t: t.matmul(
                ps_ap, acol[par][:, dc * 128:(dc + 1) * 128],
                xlnf[:, dc * TLOC + half * TT: dc * TLOC + half * TT + TT],
                start=(dc == 0), stop=(dc == DC - 1)))(), waits=w, inc="pe")
        return sch.n("pe")

    def qk_head(w_hbm, col_i, par, dest, qh, dest_guard=()):
        d_col = proj_col(w_hbm, col_i, par)
        dlast = 0
        for half in range(2):
            t0h, t1h = half * TT, (half + 1) * TT
            b64 = strip[1][0:64, t0h:t1h]
            extra = [("dve", d_col)] if half == 0 else []
            pe_p = proj_mm_half(par, half, half, extra=extra)
            if half == 1:
                setwm(("acol", par), "pe", pe_p)
            d_raw = emit("vector", (lambda half=half: lambda v: v.tensor_copy(
                praw[:, :], pp[0][:, :]))(),
                waits=[("pe", pe_p)] + wm("praw"), inc="dve")
            setwm(("pp", 0), "dve", d_raw)
            a_sq = emit("scalar", lambda s: s.activation(
                out=psq[:, :], in_=praw[:, :], func=AF.Square),
                waits=[("dve", d_raw)] + wm("psq"), inc="act")
            pz = emit("tensor", lambda t: t.matmul(
                ps_zr, onescol[:, :], psq[:, :], start=True, stop=True),
                waits=[("act", a_sq)] + wm("ps_zr"), inc="pe")
            setwm("psq", "pe", pz)
            aln = emit("scalar", lambda s: s.activation(
                out=rr1[:, :], in_=ps_zr, func=AF.Ln,
                bias=epsrms[:, :], scale=1.0 / DH),
                waits=[("pe", pz)] + wm("rr1"), inc="act")
            setwm("ps_zr", "act", aln)
            if qh is None:
                aex = emit("scalar", lambda s: s.activation(
                    out=rr2[:, :], in_=rr1[:, :], func=AF.Exp, scale=-0.5),
                    waits=[("act", aln)] + wm("rr2"), inc="act")
                setwm("rr1", "act", aex)
                row_w = [("act", aex)]
            else:
                aex = emit("scalar", lambda s: s.activation(
                    out=rr1[:, :], in_=rr1[:, :], func=AF.Exp, scale=-0.5),
                    waits=[("act", aln)], inc="act")
                d_row = emit("vector", (lambda half=half, qh=qh: lambda v: v.scalar_tensor_tensor(
                    rr2[:, :], rr1[:, :], scl[:, qh:qh + 1],
                    logp[:, half * TT:(half + 1) * TT], ALU.mult, ALU.mult))(),
                    waits=[("act", aex), ("sm2", W_SM2)] + wm("rr2"), inc="dve")
                setwm("rr1", "dve", d_row)
                row_w = [("dve", d_row)]
            pb = emit("tensor", (lambda half=half: lambda t: t.matmul(
                strip[1][0:64, half * TT:(half + 1) * TT], onesrowf[:, 0:64], rr2[:, :],
                start=True, stop=True))(),
                waits=row_w + wm(("strip", 1)), inc="pe")
            setwm("rr2", "pe", pb)
            setwm(("strip", 1), "pe", pb)
            wq2 = emit("sync", lambda s: s.dma_start(
                out=q2s[:, :], in_=praw[64:128, :]),
                waits=[("dve", d_raw)] + wm("q2s"), inc="sq2", amt=16)
            d = emit("vector", (lambda t0h=t0h, t1h=t1h: lambda v: v.tensor_mul(
                sc1[:, :], q2s[:, :], sin_t[:, t0h:t1h]))(),
                waits=[("sq2", wq2), ("sm2", W_SM2)] + wm("sc1"), inc="dve")
            d = emit("vector", (lambda t0h=t0h, t1h=t1h: lambda v: v.tensor_mul(
                sc2[:, :], praw[0:64, :], cos_t[:, t0h:t1h]))(),
                waits=wm("sc2"), inc="dve")
            d = emit("vector", lambda v: v.tensor_tensor(
                sc2[:, :], sc2[:, :], sc1[:, :], ALU.add),
                waits=[("dve", d)], inc="dve")
            dlow = emit("vector", (lambda dest=dest, t0h=t0h, t1h=t1h, b64=b64:
                                   lambda v: v.tensor_mul(
                dest[0:64, t0h:t1h], sc2[:, :], b64))(),
                waits=[("dve", d), ("pe", pb)] + list(dest_guard), inc="dve")
            d = emit("vector", (lambda t0h=t0h, t1h=t1h: lambda v: v.tensor_mul(
                sc1[:, :], praw[0:64, :], sin_t[:, t0h:t1h]))(),
                waits=[("dve", dlow)], inc="dve")
            d = emit("vector", (lambda t0h=t0h, t1h=t1h: lambda v: v.tensor_mul(
                sc2[:, :], q2s[:, :], cos_t[:, t0h:t1h]))(), inc="dve")
            d = emit("vector", lambda v: v.tensor_tensor(
                sc2[:, :], sc2[:, :], sc1[:, :], ALU.subtract),
                waits=[("dve", d)], inc="dve")
            d = emit("vector", (lambda t0h=t0h, t1h=t1h, b64=b64: lambda v: v.tensor_mul(
                y2s[:, :], sc2[:, :], b64))(),
                waits=[("dve", d)] + wm("y2s"), inc="dve")
            whigh = emit("sync", (lambda dest=dest, t0h=t0h, t1h=t1h: lambda s: s.dma_start(
                out=dest[64:128, t0h:t1h], in_=y2s[:, :]))(),
                waits=[("dve", d)], inc="ydma", amt=16)
            setwm("y2s", "ydma", whigh)
            setwm("q2s", "dve", d)
            setwm("sc1", "dve", d)
            setwm("sc2", "dve", d)
            last["praw"] = (("act", a_sq), ("dve", d), ("sq2", wq2))
            dlast = d
        addwm(("strip", 1), "dve", dlast)
        addwm(("pp", 0), "act", sch.n("act"))
        return dlast, whigh

    for kc in range(KVH):
        dest = K_all[:, kc * 2048 + 1024: kc * 2048 + 2048]
        dd, wh = qk_head(wkh, kc, kc % 2, dest, None)
        last[("kown", kc)] = (("dve", dd), ("ydma", wh))

    for dc in range(DC):
        par = dc % 2
        wv_w = emit("sync", (lambda dc=dc, par=par: lambda s: s.dma_start(
            out=afp[par][:, :], in_=wvh.ap()[dc * 128:(dc + 1) * 128, :]))(),
            waits=wm(("afp", par)), inc=f"sa{par}", amt=16)
        dv = emit("vector", (lambda dc=dc, par=par: lambda v: v.tensor_copy(
            wv_sb[:, dc * 512:(dc + 1) * 512], afp[par][:, :]))(),
            waits=[(f"sa{par}", wv_w)], inc="dve")
        setwm(("afp", par), "dve", dv)
    d_wv = sch.n("dve")
    for tc in range(8):
        vps = pp[0][:, :] if tc % 2 == 0 else strip[0][:, 0:TT]
        vkey = ("pp", 0) if tc % 2 == 0 else ("strip", 0)
        for dc in range(DC):
            w = ([("dve", d_wv)] + wm(vkey)) if dc == 0 else []
            emit("tensor", (lambda tc=tc, dc=dc, vps=vps: lambda t: t.matmul(
                vps,
                xlnf[:, dc * TLOC + tc * 128: dc * TLOC + (tc + 1) * 128],
                wv_sb[:, dc * 512:(dc + 1) * 512],
                start=(dc == 0), stop=(dc == DC - 1)))(), waits=w, inc="pe")
        pev = sch.n("pe")
        av = emit("scalar", (lambda tc=tc, vps=vps: lambda s: s.activation(
            out=V_all[:, (8 + tc) * 512:(9 + tc) * 512], in_=vps,
            func=AF.Copy))(), waits=[("pe", pev)], inc="act")
        setwm(vkey, "act", av)
        setwm(("vown", tc), "act", av)

    SGW = [0] * H
    for gh in range(H):
        gpar = gh % 2
        d_colg = proj_col(wgh, gh, gpar)
        for half in range(2):
            extra = [("dve", d_colg)] if half == 0 else []
            gsel = (gh * 2 + half) % 2
            gps = pp[0][:, :] if gsel == 0 else strip[0][:, 0:TT]
            gkey = ("pp", 0) if gsel == 0 else ("strip", 0)
            peg = proj_mm_half(gpar, half, half, extra=extra, ps=gps, pskey=gkey)
            ag = emit("scalar", (lambda half=half, gps=gps: lambda s: s.activation(
                out=sgtmp[:, :], in_=gps, func=AF.Silu))(),
                waits=[("pe", peg)] + wm("sgtmp"), inc="act")
            setwm(gkey, "act", ag)
            wsg = emit("sync", (lambda gh=gh, half=half: lambda s: s.dma_start(
                out=sgb.ap()[gh][:, half * TT:(half + 1) * TT],
                in_=sgtmp[:, :]))(), waits=[("act", ag)], inc="sgw", amt=16)
            setwm("sgtmp", "sgw", wsg)
        SGW[gh] = sch.n("sgw")
        setwm(("acol", gpar), "pe", sch.n("pe"))

    for kc in range(KVH):
        emit("gpsimd", (lambda kc=kc: lambda g: g.dma_start(
            out=kv_in.ap()[kc],
            in_=K_all[:, kc * 2048 + 1024: kc * 2048 + 2048]))(),
            waits=wm(("kown", kc)), inc="gdma", amt=16)
    for pr in range(4):
        emit("gpsimd", (lambda pr=pr: lambda g: g.dma_start(
            out=kv_in.ap()[4 + pr],
            in_=V_all[:, (8 + 2 * pr) * 512:(8 + 2 * pr + 2) * 512]))(),
            waits=wm(("vown", 2 * pr)) + wm(("vown", 2 * pr + 1)),
            inc="gdma", amt=16)
    gw = sch.n("gdma")
    c_kv = emit("gpsimd", lambda g: g.collective_compute(
        "AllGather", ALU.bypass, replica_groups=rg_pair,
        ins=[kv_in.ap()], outs=[kv_out.ap()]),
        waits=[("gdma", gw)], inc="cc")
    for kc in range(KVH):
        emit("gpsimd", (lambda kc=kc: lambda g: g.dma_start(
            out=K_all[:, kc * 2048: kc * 2048 + 1024], in_=kv_out.ap()[0, kc]))(),
            waits=[("cc", c_kv)], inc="gdma", amt=16)
    for pr in range(4):
        emit("gpsimd", (lambda pr=pr: lambda g: g.dma_start(
            out=V_all[:, 2 * pr * 512:(2 * pr + 2) * 512],
            in_=kv_out.ap()[0, 4 + pr]))(), inc="gdma", amt=16)
    W_CTXD = sch.n("gdma")
    D_CTX = 0
    for kc in range(KVH):
        D_CTX = emit("vector", (lambda kc=kc: lambda v: v.tensor_scalar(
            out=K_all[:, kc * 2048: kc * 2048 + 1024],
            in0=K_all[:, kc * 2048: kc * 2048 + 1024],
            scalar1=v01t[:, :], scalar2=None, op0=ALU.mult))(),
            waits=[("gdma", W_CTXD)], inc="dve")
    for vc in range(4):
        D_CTX = emit("vector", (lambda vc=vc: lambda v: v.tensor_scalar(
            out=V_all[:, vc * 1024:(vc + 1) * 1024],
            in0=V_all[:, vc * 1024:(vc + 1) * 1024],
            scalar1=v01t[:, :], scalar2=None, op0=ALU.mult))(),
            waits=[("gdma", W_CTXD)] if vc == 0 else [], inc="dve")
    W_CTX = sch.n("gdma")

    for h in range(H):
        kvh = h // 4
        par = h % 2
        dq, whq = qk_head(wqh, h, par, q_s[:, :], h, dest_guard=wm("q_s"))
        last["q_s_ready"] = (("dve", dq), ("ydma", whq))
        hpar = h % 2
        W_SG = emit("gpsimd", (lambda h=h, hpar=hpar: lambda g: g.dma_start(
            out=sg_h[:, hpar * TLOC:(hpar + 1) * TLOC], in_=sgb.ap()[h]))(),
            waits=[("sgw", SGW[h])] + wm(("sgh", hpar)), inc=f"sgr{hpar}", amt=16)

        S = [dict() for _ in range(8)]
        for k in range(12):
            sA = k <= 7
            sB = 2 <= k <= 9
            sC = 3 <= k <= 10
            sD = k >= 4
            qa, qb, qc, qd = k, k - 2, k - 3, k - 4
            if sD:
                zi = qd % 3
                bz = strip[qd % 2][:, 1280:1408]
                dzn = emit("vector", (lambda zi=zi, bz=bz: lambda v: v.scalar_tensor_tensor(
                    z_t[:, zi * 128:(zi + 1) * 128], z_t[:, zi * 128:(zi + 1) * 128],
                    gn[:, :], bz, ALU.mult, ALU.mult))(),
                    waits=[("pe", S[qd]["pb2"])], inc="dve")
                setwm(("bz", qd % 2), "dve", dzn)
                dxa = emit("vector", (lambda h=h, qd=qd, zi=zi: lambda v: v.tensor_tensor(
                    x[:, h * TLOC + qd * 128: h * TLOC + (qd + 1) * 128],
                    x[:, h * TLOC + qd * 128: h * TLOC + (qd + 1) * 128],
                    z_t[:, zi * 128:(zi + 1) * 128], ALU.add))(),
                    waits=[("dve", dzn)], inc="dve")
                setwm(("zt", zi), "dve", dxa)
            if sB:
                pj = qb % 3
                for j in range(8):
                    w = []
                    if j == 0:
                        w = wm("ptp") + [("act", S[qb]["a2"])]
                    emit("tensor", (lambda j=j, pj=pj: lambda t: t.transpose(
                        ptp[:, j * 128:(j + 1) * 128],
                        p_sb[:, pj * 1152 + j * 128: pj * 1152 + (j + 1) * 128],
                        ident[:, :]))(), waits=w, inc="pe")
                S[qb]["pe8a"] = sch.n("pe")
                ev1 = emit("vector", (lambda: lambda v: v.tensor_copy(
                    pT[:, 0:1024], ptp[:, :]))(),
                    waits=[("pe", S[qb]["pe8a"])] + wm("pT"), inc="dve")
                S[qb]["ev1"] = ev1
            if sC:
                zi = qc % 3
                sp = qc % 2
                zz = strip[sp][0:1, 1408:1536]
                w = [("act", S[qc]["az"])] + wm(("zq", zi))
                if qc + 2 <= 7:
                    w.append(("act", S[qc + 2]["a2"]))
                pz = emit("tensor", (lambda zi=zi, zz=zz: lambda t: t.matmul(
                    zz, onescol[:, :], zq_t[:, zi * 128:(zi + 1) * 128],
                    start=True, stop=True))(), waits=w, inc="pe")
                setwm(("zq", zi), "pe", pz)
                S[qc]["pz"] = pz
            if sB:
                pj = qb % 3
                pe8 = emit("tensor", (lambda pj=pj: lambda t: t.transpose(
                    ptp[:, 0:128],
                    p_sb[:, pj * 1152 + 8 * 128: pj * 1152 + 9 * 128],
                    ident[:, :]))(), waits=[("dve", S[qb]["ev1"])], inc="pe")
                setwm(("p_sb", pj), "pe", pe8)
                S[qb]["pe8"] = pe8
                ev = emit("vector", (lambda: lambda v: v.tensor_copy(
                    pT[:, 1024:1152], ptp[:, 0:128]))(),
                    waits=[("pe", pe8)], inc="dve")
                setwm("ptp", "dve", ev)
                S[qb]["ev"] = ev
                sp = qb % 2
                kvh_ = kvh
                ys = strip[sp][:, 1152:1280]
                for j in range(9):
                    kt = qb + j
                    w = []
                    if j == 0:
                        w = [("dve", S[qb]["ev1"])] + wm(("ys", sp))
                        if qb == 0 and h == 0:
                            w.append(("dve", D_CTX))
                    if j == 8:
                        w = [("dve", ev)]
                    emit("tensor", (lambda j=j, kt=kt, kvh_=kvh_, ys=ys:
                                    lambda t: t.matmul(
                        ys,
                        V_all[:, kt * 512 + kvh_ * 128: kt * 512 + kvh_ * 128 + 128],
                        pT[:, j * 128:(j + 1) * 128],
                        start=(j == 0), stop=(j == 8)))(), waits=w, inc="pe")
                pe_y = sch.n("pe")
                S[qb]["pe_y"] = pe_y
                setwm("pT", "pe", pe_y)
            if sA:
                sp = qa % 2
                st = strip[sp]
                qtile = q_s[:, qa * 128:(qa + 1) * 128]
                kbase = kvh * 2048 + qa * 128
                for pc, (off, wdt) in enumerate(((0, 512), (512, 512), (1024, 128))):
                    w = []
                    if pc == 0:
                        w = wm(("strip", sp))
                        if qa == 0:
                            w = w + (wm("q_s_ready")
                                     + [("gdma", W_CTX), ("dve", D_CTX),
                                        ("sm2", W_SM2)])
                    emit("tensor", (lambda st=st, off=off, wdt=wdt, qtile=qtile,
                                    kbase=kbase, pc=pc: lambda t: t.matmul(
                        st[:, off:off + wdt], qtile,
                        K_all[:, kbase + off: kbase + off + wdt],
                        start=True, stop=(pc == 1)))(), waits=w, inc="pe")
                    if pc == 0:
                        emit("tensor", (lambda st=st: lambda t: t.matmul(
                            st[:, 0:128], ident[:, :], mwin[:, :],
                            start=False, stop=True))(), inc="pe")
                    if pc == 2:
                        emit("tensor", (lambda st=st: lambda t: t.matmul(
                            st[:, 1024:1152], ident[:, :], mdiag[:, :],
                            start=False, stop=True))(), inc="pe")
                S[qa]["pe_qk"] = sch.n("pe")
            if sC:
                sp = qc % 2
                zz = strip[sp][0:1, 1408:1536]
                au = emit("scalar", (lambda sp=sp, zz=zz: lambda s: s.activation(
                    out=rr1[:, sp * 128:(sp + 1) * 128], in_=zz, func=AF.Ln,
                    bias=epsrms[:, :], scale=1.0 / DH))(),
                    waits=[("pe", S[qc]["pz"])], inc="act")
                setwm(("zz", sp), "act", au)
                ar2 = emit("scalar", (lambda sp=sp: lambda s: s.activation(
                    out=rr2[:, sp * 128:(sp + 1) * 128],
                    in_=rr1[:, sp * 128:(sp + 1) * 128], func=AF.Exp, scale=-0.5))(),
                    waits=[("act", au)], inc="act")
                pb2 = emit("tensor", (lambda sp=sp: lambda t: t.matmul(
                    strip[sp][:, 1280:1408], onesrowf[:, :],
                    rr2[:, sp * 128:(sp + 1) * 128],
                    start=True, stop=True))(),
                    waits=[("act", ar2)] + wm(("bz", sp)), inc="pe")
                S[qc]["pb2"] = pb2
            if sB:
                sp = qb % 2
                zi = qb % 3
                ys = strip[sp][:, 1152:1280]
                w = [("pe", S[qb]["pe_y"])] + wm(("zt", zi))
                if qb == 0:
                    w.append((f"sgr{hpar}", W_SG))
                dz = emit("vector", (lambda qb=qb, zi=zi, ys=ys, hpar=hpar:
                                     lambda v: v.tensor_mul(
                    z_t[:, zi * 128:(zi + 1) * 128], ys,
                    sg_h[:, hpar * TLOC + qb * 128: hpar * TLOC + (qb + 1) * 128]))(),
                    waits=w, inc="dve")
                setwm(("ys", sp), "dve", dz)
                az = emit("scalar", (lambda zi=zi: lambda s: s.activation(
                    out=zq_t[:, zi * 128:(zi + 1) * 128],
                    in_=z_t[:, zi * 128:(zi + 1) * 128], func=AF.Square))(),
                    waits=[("dve", dz)], inc="act")
                S[qb]["az"] = az
            if sA:
                sp = qa % 2
                pj = qa % 3
                st = strip[sp]
                dmx = emit("vector", (lambda st=st, sp=sp: lambda v: v.reduce_max(
                    den_t[:, sp * 4: sp * 4 + 1], st[:, 0:1152],
                    AXX, negate=True))(),
                    waits=[("pe", S[qa]["pe_qk"])] + wm(("dent", sp)), inc="dve")
                a2 = emit("scalar", (lambda st=st, sp=sp, pj=pj: lambda s: s.activation(
                    out=p_sb[:, pj * 1152: pj * 1152 + 1152], in_=st[:, 0:1152],
                    func=AF.Exp, bias=den_t[:, sp * 4: sp * 4 + 1], scale=1.0))(),
                    waits=[("dve", dmx)] + wm(("p_sb", pj)), inc="act")
                setwm(("strip", sp), "act", a2)
                setwm(("dent", sp), "act", a2)
                S[qa]["a2"] = a2

        setwm("q_s", "pe", sch.n("pe"))
        setwm(("sgh", hpar), "dve", sch.n("dve"))

    astk.close()
    sch.barrier()
    clear_ffn_keys()

    with contextlib.ExitStack() as fstk:
        ft2 = ffn_alloc(fstk, "f2", False)
        ffn_sweep(ft2, 0, quant=False)
        ffn_sweep(ft2, 1, quant=False)

    for c in range(DC):
        w = wm(("xout", 0, c)) + wm(("xout", TT, c))
        emit("gpsimd", (lambda c=c: lambda g: g.dma_start(
            out=out_fm.ap()[c * 128:(c + 1) * 128, :],
            in_=x[:, c * TLOC:(c + 1) * TLOC]))(), waits=w, inc="gdma", amt=16)
    sch.ops["gpsimd"].append((None, (("gdma", sch.n("gdma")),), None, 0, None))

    with nc.Block() as block:
        @block.sync
        def _(e):
            sch.replay("sync", e, sems)

        @block.tensor
        def _(e):
            sch.replay("tensor", e, sems)

        @block.vector
        def _(e):
            sch.replay("vector", e, sems)

        @block.scalar
        def _(e):
            sch.replay("scalar", e, sems)

        @block.gpsimd
        def _(e):
            sch.replay("gpsimd", e, sems)

    ctx.close()
    return nc


def _stage(inputs, n_cores=8):
    x = np.asarray(inputs["L0_attn_hidden_states"], dtype=np.float32)
    wq = np.asarray(inputs["L0_attn_Wq"], np.float32)
    wk = np.asarray(inputs["L0_attn_Wk"], np.float32)
    wv = np.asarray(inputs["L0_attn_Wv"], np.float32)
    wg = np.asarray(inputs["L0_attn_Wg"], np.float32)
    sscal = np.asarray(inputs["L0_attn_softmax_scaler"], np.float32)
    gnw = np.asarray(inputs["L0_attn_gn_weight"], np.float32)
    wu = np.asarray(inputs["L0_ffn_W_up"], np.float32)
    wgt = np.asarray(inputs["L0_ffn_W_gate"], np.float32)
    wd = np.asarray(inputs["L0_ffn_W_down"], np.float32)

    def colmajor_up(w):
        return np.ascontiguousarray(
            w.reshape(EC, 128, DC, 128).transpose(0, 3, 2, 1)).reshape(EC, 128, 2048)

    tu_t = colmajor_up(wu)
    tg_t = colmajor_up(wgt)
    td_t = np.ascontiguousarray(
        wd.reshape(DC, 128, EC, 128).transpose(0, 3, 2, 1)).reshape(DC, 128, 8192)

    def headcols(w, nh):
        return np.ascontiguousarray(
            w.reshape(DC, 128, nh, 128).transpose(2, 1, 0, 3)).reshape(nh, 128, 2048)

    wqh = headcols(wq, H)
    wkh = headcols(wk, KVH)
    wgh = headcols(wg, H)

    shards = [w.reshape(8, 128, 16384) for w in (wu, wgt, wd)]
    inv_freq = (1.0 / (10000.0 ** (np.arange(0, DH, 2, dtype=np.float32) / DH))).astype(np.float32)

    ii = np.arange(128)[:, None]
    jj = np.arange(128)[None, :]
    masks = np.stack([
        np.where(jj >= ii, 0.0, NEGBIG).astype(np.float32),
        np.where(jj <= ii, 0.0, NEGBIG).astype(np.float32)]).astype(bf16)

    in_maps = []
    for c in range(n_cores):
        b, half = c // 2, c % 2
        t0 = half * TLOC
        pos = np.arange(t0, t0 + TLOC, dtype=np.float32)
        freqs = np.outer(pos, inv_freq).astype(np.float32)
        cosin = np.ascontiguousarray(
            np.stack([np.cos(freqs).T, np.sin(freqs).T])).astype(bf16)
        im = {
            "x_fm": np.ascontiguousarray(x[b, t0:t0 + TLOC, :].T),
            "wqh": wqh, "wkh": wkh, "wvh": wv, "wgh": wgh,
            "tu_t": tu_t, "tg_t": tg_t, "td_t": td_t,
            "shard": np.ascontiguousarray(
                np.stack([shards[0][c % 8], shards[1][c % 8], shards[2][c % 8]])),
            "cosin": cosin,
            "logpos": np.log(pos + 1.0).astype(bf16)[None, :],
            "scaler": sscal[None, :].astype(np.float32),
            "gnw": gnw[:, None].astype(np.float32),
            "v01": np.full((128, 1), 1.0 if half == 1 else 0.0, np.float32),
            "masks": masks, "identw": np.eye(128, dtype=bf16),
        }
        in_maps.append(im)
    return in_maps


def kernel(**inputs):
    in_maps = _stage(inputs, 8)
    nc = build_nc(8)
    res = run_bass_kernel_spmd(nc, in_maps, core_ids=list(range(8)))
    out = np.empty((4, 2048, 2048), dtype=np.float32)
    for c in range(8):
        b, half = c // 2, c % 2
        out[b, half * TLOC:(half + 1) * TLOC, :] = np.asarray(res.results[c]["out_fm"]).T
    return out


# version 17
# speedup vs baseline: 1.0388x; 1.0388x over previous
import contextlib
import numpy as np
import ml_dtypes
import concourse.bass as bass
import concourse.mybir as mybir
from concourse.bass_utils import run_bass_kernel_spmd

F32 = mybir.dt.float32
BF16 = mybir.dt.bfloat16
AF = mybir.ActivationFunctionType
ALU = mybir.AluOpType
AXX = mybir.AxisListType.X
bf16 = ml_dtypes.bfloat16

D, E, DC, EC = 2048, 8192, 16, 64
TLOC, TT = 1024, 512
H, KVH, DH = 16, 4, 128
NW = float(D * E)
NEGBIG = -1.0e30


class Sched:
    ENGINES = ("sync", "tensor", "vector", "scalar", "gpsimd")

    def __init__(self):
        self.ops = {e: [] for e in self.ENGINES}
        self.counts = {}

    def emit(self, eng, fn, waits=(), inc=None, amt=1):
        tgt = None
        if inc is not None:
            self.counts[inc] = self.counts.get(inc, 0) + amt
            tgt = self.counts[inc]
        ws = tuple((s, v) for (s, v) in waits if v)
        self.ops[eng].append((fn, ws, inc, amt, tgt))
        return tgt

    def n(self, sem):
        return self.counts.get(sem, 0)

    def barrier(self):
        snap = [(s, v) for s, v in self.counts.items() if v > 0]
        for e in self.ENGINES:
            self.ops[e].append((None, tuple(snap), None, 0, None))

    CHAIN = {"vector": "dve", "scalar": "act"}

    def replay(self, name, eng, sems):
        chain_sem = self.CHAIN.get(name)
        prev = 0
        for fn, waits, inc, amt, tgt in self.ops[name]:
            for s, v in waits:
                if chain_sem is not None and s == chain_sem and v <= prev:
                    continue
                eng.wait_ge(sems[s], v)
            if fn is None:
                continue
            if chain_sem is not None and prev > 0:
                eng.wait_ge(sems[chain_sem], prev)
            inst = fn(eng)
            if inc is not None:
                inst.then_inc(sems[inc], amt)
                if inc == chain_sem:
                    prev = tgt


def build_nc(n_cores):
    nc = bass.Bass("TRN2", target_bir_lowering=False)
    sch = Sched()
    emit = sch.emit

    DT = nc.dram_tensor
    x_fm = DT("x_fm", [D, TLOC], F32, kind="ExternalInput")
    wqh = DT("wqh", [H, 128, 2048], F32, kind="ExternalInput")
    wkh = DT("wkh", [KVH, 128, 2048], F32, kind="ExternalInput")
    wvh = DT("wvh", [D, 512], F32, kind="ExternalInput")
    wgh = DT("wgh", [H, 128, 2048], F32, kind="ExternalInput")
    tu_t = DT("tu_t", [EC, 128, 2048], F32, kind="ExternalInput")
    tg_t = DT("tg_t", [EC, 128, 2048], F32, kind="ExternalInput")
    td_t = DT("td_t", [DC, 128, 8192], F32, kind="ExternalInput")
    shard = DT("shard", [3, 128, 16384], F32, kind="ExternalInput")
    cosin = DT("cosin", [2, 64, TLOC], BF16, kind="ExternalInput")
    logpos = DT("logpos", [1, TLOC], BF16, kind="ExternalInput")
    scaler = DT("scaler", [1, H], F32, kind="ExternalInput")
    gnw = DT("gnw", [128, 1], F32, kind="ExternalInput")
    v01 = DT("v01", [128, 1], F32, kind="ExternalInput")
    masks = DT("masks", [2, 128, 128], BF16, kind="ExternalInput")
    identw = DT("identw", [128, 128], BF16, kind="ExternalInput")
    out_fm = DT("out_fm", [D, TLOC], F32, kind="ExternalOutput")
    tub = DT("tub", [EC, 128, 2048], BF16)
    tgb = DT("tgb", [EC, 128, 2048], BF16)
    tdb = DT("tdb", [DC, 128, 8192], BF16)
    sc_in = DT("sc_in", [1, 4], F32)
    sc_out = DT("sc_out", [1, 4], F32)
    sgb = DT("sgb", [H, 128, TLOC], BF16)
    kv_in = DT("kv_in", [8, 128, 1024], BF16)
    kv_out = DT("kv_out", [2, 8, 128, 1024], BF16)

    rg_pair = [[2 * i, 2 * i + 1] for i in range(max(1, n_cores // 2))]
    rg_all = [list(range(n_cores))]

    ctx = contextlib.ExitStack()
    sem_names = ["gdma", "pe", "dve", "act", "cc", "xdma", "sm1", "sm2",
                 "su0", "su1", "sg0", "sg1", "sd0", "sd1", "sw0", "sw1",
                 "cu0", "cu1", "cg0", "cg1", "cd0", "cd1",
                 "sa0", "sa1", "sq2", "ydma", "tdma", "sgw", "sgr0", "sgr1"]
    sems = {s: ctx.enter_context(nc.semaphore(s)) for s in sem_names}

    def sb(name, shape, dt):
        return ctx.enter_context(nc.sbuf_tensor(name, shape, dt))

    def psb(name, shape):
        return ctx.enter_context(nc.psum_tensor(name, shape, F32))

    x = sb("x", [128, DC * TLOC], F32)
    consts = sb("consts", [128, 16], F32)
    onescol = sb("onescol", [128, 1], BF16)
    onescolf = sb("onescolf", [128, 1], F32)
    onesrow = sb("onesrow", [1, 128], BF16)
    onesrowf = sb("onesrowf", [1, 128], F32)
    epsln = sb("epsln", [1, 1], F32)
    epsrms = sb("epsrms", [1, 1], F32)
    gn = sb("gn", [128, 1], F32)
    v01t = sb("v01t", [128, 1], F32)
    scl = sb("scl", [1, H], F32)
    vals = sb("vals", [1, 16], F32)
    totals = sb("totals", [1, 4], F32)
    redt = sb("redt", [128, 4], F32)
    rowa = sb("rowa", [1, TT], F32)
    rowb = sb("rowb", [1, TT], F32)
    rowc = sb("rowc", [1, TT], BF16)
    rowd = sb("rowd", [1, TT], BF16)
    xbt = sb("xbt", [128, TT], BF16)
    xsqt = sb("xsqt", [128, TT], BF16)
    t1t = sb("t1t", [128, TT], F32)
    den_c = sb("den_c", [128, 2], F32)
    den_o = sb("den_o", [128, 2], F32)
    den_t = sb("den_t", [128, 8], F32)
    p0stk = contextlib.ExitStack()
    ps_tot = p0stk.enter_context(nc.psum_tensor("ps_tot", [1, 4], F32))
    ps_bc = p0stk.enter_context(nc.psum_tensor("ps_bc", [128, 16], F32))

    SUP, SGATE, HSD, NTU, NTG, NTD, PTU, PTG, PTD = 0, 1, 2, 3, 4, 5, 6, 7, 8

    last = {}

    def wm(key):
        return list(last.get(key, ()))

    def setwm(key, sem, val):
        last[key] = ((sem, val),)

    def addwm(key, sem, val):
        last[key] = tuple(last.get(key, ())) + ((sem, val),)

    def _init1(v):
        v.memset(totals[:, :], 0.0)
        v.memset(vals[:, :], 0.0)
        v.memset(onescol[:, :], 1.0)
        v.memset(onescolf[:, :], 1.0)
        v.memset(onesrow[:, :], 1.0)
        v.memset(onesrowf[:, :], 1.0)
        v.memset(epsln[:, :], 1e-5)
        return v.memset(epsrms[:, :], 1e-6)
    d_init = emit("vector", _init1, inc="dve")

    for dst_, src_ in ((gn, gnw), (v01t, v01), (scl, scaler)):
        W_SCL = emit("sync", (lambda dst_=dst_, src_=src_: lambda s: s.dma_start(
            out=dst_[:, :], in_=src_.ap()))(), inc="sm1", amt=16)
    emit("vector", lambda v: v.tensor_scalar(
        out=scl[:, :], in0=scl[:, :], scalar1=1.0 / float(np.sqrt(DH)),
        scalar2=None, op0=ALU.mult), waits=[("sm1", W_SCL)], inc="dve")

    for c in range(DC):
        emit("sync", (lambda c=c: lambda s: s.dma_start(
            out=x[:, c * TLOC:(c + 1) * TLOC],
            in_=x_fm.ap()[c * 128:(c + 1) * 128, :]))(), inc="xdma", amt=16)
    W_X = sch.n("xdma")

    with nc.sbuf_tensor("shard_buf", [128, 16384], F32) as shard_buf:
        dprev = d_init
        for m in range(3):
            wg_ = emit("gpsimd", (lambda m=m: lambda g: g.dma_start(
                out=shard_buf[:, :], in_=shard.ap()[m]))(),
                waits=[("dve", dprev)], inc="gdma", amt=16)
            dprev = emit("vector", (lambda m=m: lambda v: v.tensor_reduce(
                redt[:, m:m + 1], shard_buf[:, :], AXX, ALU.add,
                apply_absolute_value=True))(),
                waits=[("gdma", wg_)], inc="dve")
    p_tot = emit("tensor", lambda t: t.matmul(
        ps_tot[:, 0:3], onescolf[:, :], redt[:, 0:3], start=True, stop=True),
        waits=[("dve", dprev)], inc="pe")
    d_tot = emit("vector", lambda v: v.tensor_copy(totals[:, 0:3], ps_tot[:, 0:3]),
                 waits=[("pe", p_tot)], inc="dve")
    wg_ = emit("gpsimd", lambda g: g.dma_start(out=sc_in.ap(), in_=totals[:, 0:4]),
               waits=[("dve", d_tot)], inc="gdma", amt=16)
    c_ar = emit("gpsimd", lambda g: g.collective_compute(
        "AllReduce", ALU.add, replica_groups=rg_all,
        ins=[sc_in.ap()], outs=[sc_out.ap()]),
        waits=[("gdma", wg_)], inc="cc")
    wg_ = emit("gpsimd", lambda g: g.dma_start(out=totals[:, 0:4], in_=sc_out.ap()),
               waits=[("cc", c_ar)], inc="gdma", amt=16)

    d_v = emit("vector", lambda v: v.tensor_scalar(
        out=rowa[:, 0:3], in0=totals[:, 0:3],
        scalar1=1.0 / NW, scalar2=1e-8, op0=ALU.mult, op1=ALU.add),
        waits=[("gdma", wg_)], inc="dve")
    d_v = emit("vector", lambda v: v.tensor_scalar(
        out=vals[:, SUP:SUP + 2], in0=rowa[:, 0:2],
        scalar1=0.5, scalar2=None, op0=ALU.mult),
        waits=[("dve", d_v)], inc="dve")
    d_v = emit("vector", lambda v: v.tensor_scalar(
        out=vals[:, HSD:HSD + 1], in0=rowa[:, 2:3],
        scalar1=0.25, scalar2=None, op0=ALU.mult),
        waits=[("dve", d_v)], inc="dve")
    d_v = emit("vector", lambda v: v.tensor_scalar(
        out=vals[:, NTU:NTU + 3], in0=rowa[:, 0:3],
        scalar1=-0.5, scalar2=None, op0=ALU.mult),
        waits=[("dve", d_v)], inc="dve")
    d_v = emit("vector", lambda v: v.tensor_scalar(
        out=vals[:, PTU:PTU + 3], in0=rowa[:, 0:3],
        scalar1=0.5, scalar2=None, op0=ALU.mult),
        waits=[("dve", d_v)], inc="dve")
    p_bc = emit("tensor", lambda t: t.matmul(
        ps_bc[:, :], onesrowf[:, :], vals[:, :], start=True, stop=True),
        waits=[("dve", d_v)], inc="pe")
    D_CONSTS = emit("vector", lambda v: v.tensor_copy(consts[:, :], ps_bc[:, :]),
                    waits=[("pe", p_bc)], inc="dve")
    p0stk.close()

    sch.barrier()

    def layernorm(t0, dst, ps_r1, ps_r2, ps_v1, ps_v2, pre=()):
        d2 = pe = 0
        for c in range(DC):
            xs = x[:, c * TLOC + t0: c * TLOC + t0 + TT]
            d1 = emit("vector", (lambda xs=xs: lambda v: v.tensor_copy(xbt[:, :], xs))(),
                      waits=wm("xbt") + ([("xdma", W_X)] if t0 == 0 else []), inc="dve")
            d2 = emit("vector", lambda v: v.tensor_mul(xsqt[:, :], xbt[:, :], xbt[:, :]),
                      waits=[("dve", d1)] + wm("xsqt"), inc="dve")
            w0 = [("dve", d2)] + (wm("ln_rows") + list(pre) if c == 0 else [])
            emit("tensor", (lambda c=c: lambda t: t.matmul(
                ps_r1, onescol[:, :], xbt[:, :],
                start=(c == 0), stop=(c == DC - 1)))(), waits=w0, inc="pe")
            pe = emit("tensor", (lambda c=c: lambda t: t.matmul(
                ps_r2, onescol[:, :], xsqt[:, :],
                start=(c == 0), stop=(c == DC - 1)))(), inc="pe")
            setwm("xbt", "pe", pe)
            setwm("xsqt", "pe", pe)
        d = emit("vector", lambda v: v.tensor_scalar(
            out=rowa[:, :], in0=ps_r1, scalar1=1.0 / D, scalar2=None, op0=ALU.mult),
            waits=[("pe", pe)] + wm("rowa"), inc="dve")
        d = emit("vector", lambda v: v.tensor_mul(rowb[:, :], rowa[:, :], rowa[:, :]),
                 waits=[("dve", d)], inc="dve")
        d = emit("vector", lambda v: v.scalar_tensor_tensor(
            rowb[:, :], ps_r2, 1.0 / D, rowb[:, :], ALU.mult, ALU.subtract),
            waits=[("dve", d)], inc="dve")
        setwm("ln_rows", "dve", d)
        a0 = emit("scalar", lambda s: s.activation(
            out=rowb[:, :], in_=rowb[:, :], func=AF.Ln,
            bias=epsln[:, :], scale=1.0), waits=[("dve", d)], inc="act")
        a = emit("scalar", lambda s: s.activation(
            out=rowc[:, :], in_=rowb[:, :], func=AF.Exp, scale=-0.5),
            waits=[("act", a0)] + wm("rowc"), inc="act")
        d = emit("vector", lambda v: v.tensor_mul(rowd[:, :], rowa[:, :], rowc[:, :]),
                 waits=[("act", a)] + wm("rowd"), inc="dve")
        setwm("rowa", "dve", d)
        emit("tensor", lambda t: t.matmul(
            ps_v1, onesrow[:, :], rowc[:, :], start=True, stop=True),
            waits=[("dve", d)] + wm("ln_bc") + list(pre), inc="pe")
        pe = emit("tensor", lambda t: t.matmul(
            ps_v2, onesrow[:, :], rowd[:, :], start=True, stop=True), inc="pe")
        setwm("rowc", "pe", pe)
        setwm("rowd", "pe", pe)
        dl = 0
        for c in range(DC):
            xs = x[:, c * TLOC + t0: c * TLOC + t0 + TT]
            d1 = emit("vector", (lambda xs=xs: lambda v: v.tensor_mul(
                t1t[:, :], xs, ps_v1))(), waits=[("pe", pe), ("dve", dl)], inc="dve")
            dl = emit("vector", (lambda c=c: lambda v: v.tensor_tensor(
                dst(c), t1t[:, :], ps_v2, ALU.subtract))(),
                waits=[("dve", d1)], inc="dve")
        setwm("ln_bc", "dve", dl)
        return dl

    qctr = [0]

    def quant_half(ft, dst_ap, dst_key, ntau_idx, src_hbm_ap):
        qp = qctr[0] % 4
        qs = qp % 2
        act_path = qp < 2
        qctr[0] += 1
        sw = f"sw{qs}"
        wv_ = emit("sync", (lambda src_hbm_ap=src_hbm_ap, qp=qp: lambda s: s.dma_start(
            out=ft["wfp"][qp][:, :], in_=src_hbm_ap))(),
            waits=wm(("wfp", qp)), inc=sw, amt=16)
        ntau = consts[:, ntau_idx:ntau_idx + 1]
        ptau = consts[:, ntau_idx + 3:ntau_idx + 4]
        if act_path:
            a = emit("scalar", (lambda qp=qp, qs=qs, ntau=ntau: lambda s: s.activation(
                out=ft["qa"][qs][:, :], in_=ft["wfp"][qp][:, :], func=AF.Sign,
                bias=ntau, scale=1.0))(),
                waits=[(sw, wv_), ("dve", D_CONSTS)] + wm(("qa", qs)), inc="act")
            c = emit("scalar", (lambda qp=qp, qs=qs, ptau=ptau: lambda s: s.activation(
                out=ft["qb"][qs][:, :], in_=ft["wfp"][qp][:, :], func=AF.Sign,
                bias=ptau, scale=1.0))(),
                waits=wm(("qb", qs)), inc="act")
            d3 = emit("vector", (lambda dst_ap=dst_ap, qs=qs: lambda v: v.tensor_tensor(
                dst_ap, ft["qa"][qs][:, :], ft["qb"][qs][:, :], ALU.add))(),
                waits=[("act", c)] + wm(dst_key), inc="dve")
            setwm(("qa", qs), "dve", d3)
            setwm(("qb", qs), "dve", d3)
            last[("wfp", qp)] = (("act", c),)
        else:
            d1 = emit("vector", (lambda qp=qp, qs=qs, ptau=ptau: lambda v: v.tensor_scalar(
                out=ft["qa"][qs][:, :], in0=ft["wfp"][qp][:, :], scalar1=ptau,
                scalar2=2.0, op0=ALU.is_gt, op1=ALU.mult))(),
                waits=[(sw, wv_), ("dve", D_CONSTS)] + wm(("qa", qs)), inc="dve")
            d2 = emit("vector", (lambda qp=qp, qs=qs, ntau=ntau: lambda v: v.tensor_scalar(
                out=ft["qb"][qs][:, :], in0=ft["wfp"][qp][:, :], scalar1=ntau,
                scalar2=2.0, op0=ALU.is_lt, op1=ALU.mult))(),
                waits=wm(("qb", qs)), inc="dve")
            d3 = emit("vector", (lambda dst_ap=dst_ap, qs=qs: lambda v: v.tensor_tensor(
                dst_ap, ft["qa"][qs][:, :], ft["qb"][qs][:, :], ALU.subtract))(),
                waits=[("dve", d2)] + wm(dst_key), inc="dve")
            setwm(("qa", qs), "dve", d3)
            setwm(("qb", qs), "dve", d3)
            last[("wfp", qp)] = (("dve", d2),)
        return d3

    def stream_col(ft, kind, src_fp, cache, idx, dstbuf, par, ntau_idx, quant,
                   cache_slice=None):
        key = ("col", kind, par)
        if quant:
            mmw = []
            for h2 in range(2):
                d3 = quant_half(
                    ft, dstbuf[par][:, h2 * 1024:(h2 + 1) * 1024], key, ntau_idx,
                    src_fp[:, h2 * 1024:(h2 + 1) * 1024])
                mmw.append(("dve", d3))
            cs = f"c{kind}{par}"
            cw = emit("sync", (lambda cache_slice=cache_slice, dstbuf=dstbuf, par=par:
                               lambda s: s.dma_start(
                                   out=cache_slice, in_=dstbuf[par][:, :]))(),
                      waits=[mmw[-1]], inc=cs, amt=16)
            last[("cache", kind, par)] = ((cs, cw),)
            return mmw, key
        else:
            cs = f"s{kind}{par}"
            wv_ = emit("sync", (lambda cache_slice=cache_slice, dstbuf=dstbuf, par=par:
                                lambda s: s.dma_start(
                                    out=dstbuf[par][:, :], in_=cache_slice))(),
                       waits=wm(key), inc=cs, amt=16)
            return [(cs, wv_)], key

    def ffn_sweep(ft, sw, quant):
        t0 = sw * TT
        pre = (wm(("ps_up", 0)) + wm(("ps_up", 1)) +
               wm(("ps_gt", 0)) + wm(("ps_gt", 1)))
        d_ln = layernorm(t0, lambda c: ft["xln"][:, c * TT:(c + 1) * TT],
                         ft["ps_up"][0][0:1, :], ft["ps_gt"][0][0:1, :],
                         ft["ps_up"][1][:, :], ft["ps_gt"][1][:, :], pre=pre)
        for i in range(2):
            setwm(("ps_up", i), "dve", d_ln)
            setwm(("ps_gt", i), "dve", d_ln)

        d_hid = 0
        for ec in range(EC):
            par = ec % 2
            mmw_u, key_u = stream_col(ft, "u", tu_t.ap()[ec], tub, ec, ft["ucol"],
                                      par, NTU, quant, tub.ap()[ec])
            mmw_g, key_g = stream_col(ft, "g", tg_t.ap()[ec], tgb, ec, ft["gcol"],
                                      par, NTG, quant, tgb.ap()[ec])
            for dc in range(DC):
                w = []
                if dc == 0:
                    w = [mmw_u[0]] + wm(("ps_up", par)) + [("dve", d_ln)]
                elif dc == 8:
                    w = [mmw_u[-1]]
                emit("tensor", (lambda dc=dc, par=par: lambda t: t.matmul(
                    ft["ps_up"][par][:, :], ft["ucol"][par][:, dc * 128:(dc + 1) * 128],
                    ft["xln"][:, dc * TT:(dc + 1) * TT],
                    start=(dc == 0), stop=(dc == DC - 1)))(), waits=w, inc="pe")
            pe_u = sch.n("pe")
            if quant:
                for wi in range(6):
                    emit("tensor", (lambda par=par, wi=wi: lambda t: t.matmul(
                        ft["ps_wm"][:, :],
                        ft["ucol"][par][:, (wi % 16) * 128:((wi % 16) + 1) * 128],
                        ft["xln"][:, 0:TT], start=True, stop=True))(), inc="pe")
                pe_u = sch.n("pe")
            for dc in range(DC):
                w = []
                if dc == 0:
                    w = [mmw_g[0]] + wm(("ps_gt", par))
                elif dc == 8:
                    w = [mmw_g[-1]]
                emit("tensor", (lambda dc=dc, par=par: lambda t: t.matmul(
                    ft["ps_gt"][par][:, :], ft["gcol"][par][:, dc * 128:(dc + 1) * 128],
                    ft["xln"][:, dc * TT:(dc + 1) * TT],
                    start=(dc == 0), stop=(dc == DC - 1)))(), waits=w, inc="pe")
            pe_g = sch.n("pe")
            last[key_u] = (("pe", pe_u),) + last.get(("cache", "u", par), ())
            last[key_g] = (("pe", pe_g),) + last.get(("cache", "g", par), ())
            a1 = emit("scalar", (lambda par=par: lambda s: s.activation(
                out=ft["hsil"][:, :], in_=ft["ps_up"][par][:, :], func=AF.Silu,
                scale=consts[:, SUP:SUP + 1]))(),
                waits=[("pe", pe_u)] + wm("hsil"), inc="act")
            a2 = emit("scalar", (lambda par=par, ec=ec: lambda s: s.activation(
                out=ft["hid"][:, ec * TT:(ec + 1) * TT], in_=ft["ps_gt"][par][:, :],
                func=AF.Copy, scale=consts[:, SGATE:SGATE + 1]))(),
                waits=[("pe", pe_g)], inc="act")
            setwm(("ps_up", par), "act", a1)
            setwm(("ps_gt", par), "act", a2)
            d_hid = emit("vector", (lambda ec=ec: lambda v: v.tensor_mul(
                ft["hid"][:, ec * TT:(ec + 1) * TT], ft["hsil"][:, :],
                ft["hid"][:, ec * TT:(ec + 1) * TT]))(),
                waits=[("act", a2), ("act", a1)], inc="dve")
            setwm("hsil", "dve", d_hid)

        for mc in range(DC):
            par = mc % 2
            pe_d = 0
            for q in range(4):
                qp = (4 * mc + q) % 2
                key_d = ("col", "d", qp)
                if quant:
                    mmw = []
                    for h2 in range(2):
                        off = (2 * q + h2) * 1024
                        d3 = quant_half(ft, ft["dcol"][qp][:, h2 * 1024:(h2 + 1) * 1024],
                                        key_d, NTD, td_t.ap()[mc][:, off:off + 1024])
                        mmw.append(("dve", d3))
                    cw = emit("sync", (lambda mc=mc, q=q, qp=qp: lambda s: s.dma_start(
                        out=tdb.ap()[mc][:, q * 2048:(q + 1) * 2048],
                        in_=ft["dcol"][qp][:, :]))(), waits=[mmw[-1]],
                        inc=f"cd{qp}", amt=16)
                    cextra = ((f"cd{qp}", cw),)
                else:
                    wv_ = emit("sync", (lambda mc=mc, q=q, qp=qp: lambda s: s.dma_start(
                        out=ft["dcol"][qp][:, :],
                        in_=tdb.ap()[mc][:, q * 2048:(q + 1) * 2048]))(),
                        waits=wm(key_d), inc=f"sd{qp}", amt=16)
                    mmw = [(f"sd{qp}", wv_)]
                    cextra = ()
                for j in range(16):
                    ec = q * 16 + j
                    w = []
                    if j == 0:
                        w = [mmw[0]]
                        if q == 0:
                            w += [("dve", d_hid)] + wm(("ps_dn", par))
                    elif j == 8:
                        w = [mmw[-1]]
                    emit("tensor", (lambda j=j, ec=ec, par=par, qp=qp: lambda t: t.matmul(
                        ft["ps_dn"][par][:, :], ft["dcol"][qp][:, j * 128:(j + 1) * 128],
                        ft["hid"][:, ec * TT:(ec + 1) * TT],
                        start=(ec == 0), stop=(ec == EC - 1)))(), waits=w, inc="pe")
                pe_d = sch.n("pe")
                last[key_d] = (("pe", pe_d),) + cextra
            dr = emit("vector", (lambda mc=mc, par=par, t0=t0: lambda v: v.scalar_tensor_tensor(
                x[:, mc * TLOC + t0: mc * TLOC + t0 + TT],
                ft["ps_dn"][par][:, :], consts[:, HSD:HSD + 1],
                x[:, mc * TLOC + t0: mc * TLOC + t0 + TT],
                ALU.mult, ALU.add))(), waits=[("pe", pe_d)], inc="dve")
            setwm(("ps_dn", par), "dve", dr)
            setwm(("xout", t0, mc), "dve", dr)

    def ffn_alloc(stk, pfx, with_quant):
        t = {}
        t["xln"] = stk.enter_context(nc.sbuf_tensor(pfx + "xln", [128, DC * TT], BF16))
        t["hid"] = stk.enter_context(nc.sbuf_tensor(pfx + "hid", [128, EC * TT], BF16))
        t["hsil"] = stk.enter_context(nc.sbuf_tensor(pfx + "hsl", [128, TT], BF16))
        t["ucol"] = [stk.enter_context(nc.sbuf_tensor(pfx + f"uc{i}", [128, 2048], BF16)) for i in range(2)]
        t["gcol"] = [stk.enter_context(nc.sbuf_tensor(pfx + f"gc{i}", [128, 2048], BF16)) for i in range(2)]
        t["dcol"] = [stk.enter_context(nc.sbuf_tensor(pfx + f"dc{i}", [128, 2048], BF16)) for i in range(2)]
        if with_quant:
            t["wfp"] = [stk.enter_context(nc.sbuf_tensor(pfx + f"wf{i}", [128, 1024], F32)) for i in range(4)]
            t["qa"] = [stk.enter_context(nc.sbuf_tensor(pfx + f"qa{i}", [128, 1024], BF16)) for i in range(2)]
            t["qb"] = [stk.enter_context(nc.sbuf_tensor(pfx + f"qb{i}", [128, 1024], BF16)) for i in range(2)]
        if with_quant:
            t["ps_wm"] = stk.enter_context(nc.psum_tensor(pfx + "pwm", [128, TT], F32))
        t["ps_up"] = [stk.enter_context(nc.psum_tensor(pfx + f"pu{i}", [128, TT], F32)) for i in range(2)]
        t["ps_gt"] = [stk.enter_context(nc.psum_tensor(pfx + f"pg{i}", [128, TT], F32)) for i in range(2)]
        t["ps_dn"] = [stk.enter_context(nc.psum_tensor(pfx + f"pd{i}", [128, TT], F32)) for i in range(2)]
        return t

    def clear_ffn_keys():
        for key in list(last.keys()):
            if isinstance(key, tuple) and key and key[0] in (
                    "ps_up", "ps_gt", "ps_dn", "col", "cache", "wfp", "qa"):
                del last[key]

    with contextlib.ExitStack() as fstk:
        ft = ffn_alloc(fstk, "f1", True)
        ffn_sweep(ft, 0, quant=True)
        ffn_sweep(ft, 1, quant=False)
    sch.barrier()
    clear_ffn_keys()

    astk = contextlib.ExitStack()
    s2 = astk.enter_context
    xlnf = s2(nc.sbuf_tensor("a_xln", [128, DC * TLOC], BF16))
    cos_t = s2(nc.sbuf_tensor("a_cos", [64, TLOC], BF16))
    sin_t = s2(nc.sbuf_tensor("a_sin", [64, TLOC], BF16))
    logp = s2(nc.sbuf_tensor("a_logp", [1, TLOC], BF16))
    mwin = s2(nc.sbuf_tensor("a_mwin", [128, 128], BF16))
    mdiag = s2(nc.sbuf_tensor("a_mdg", [128, 128], BF16))
    ident = s2(nc.sbuf_tensor("a_id", [128, 128], BF16))
    K_all = s2(nc.sbuf_tensor("a_K", [128, KVH * 2048], BF16))
    V_all = s2(nc.sbuf_tensor("a_V", [128, 16 * 512], BF16))
    wv_sb = s2(nc.sbuf_tensor("a_wv", [128, DC * 512], BF16))
    afp = [s2(nc.sbuf_tensor(f"a_fp{i}", [128, 512], F32)) for i in range(2)]
    acol = [s2(nc.sbuf_tensor(f"a_cl{i}", [128, 2048], BF16)) for i in range(2)]
    praw = s2(nc.sbuf_tensor("a_praw", [128, TT], F32))
    sgtmp = s2(nc.sbuf_tensor("a_sgt", [128, TT], BF16))
    q2s = s2(nc.sbuf_tensor("a_q2s", [64, TT], F32))
    sc1 = s2(nc.sbuf_tensor("a_sc1", [64, TT], F32))
    sc2 = s2(nc.sbuf_tensor("a_sc2", [64, TT], F32))
    y2s = s2(nc.sbuf_tensor("a_y2s", [64, TT], BF16))
    psq = s2(nc.sbuf_tensor("a_psq", [128, TT], BF16))
    q_s = s2(nc.sbuf_tensor("a_qs", [128, TLOC], BF16))
    sg_h = s2(nc.sbuf_tensor("a_sg", [128, 2 * TLOC], BF16))
    p_sb = s2(nc.sbuf_tensor("a_p", [128, 3 * 1152], BF16))
    pT = s2(nc.sbuf_tensor("a_pT", [128, 1152], BF16))
    z_t = s2(nc.sbuf_tensor("a_z", [128, 384], F32))
    zq_t = s2(nc.sbuf_tensor("a_zq", [128, 384], BF16))
    rr1 = s2(nc.sbuf_tensor("a_r1", [1, TT], F32))
    rr2 = s2(nc.sbuf_tensor("a_r2", [1, TT], F32))
    pp = [s2(nc.psum_tensor("a_pp0", [128, TT], F32))]
    ptp = s2(nc.psum_tensor("a_ptp", [128, 1024], BF16))
    strip = [s2(nc.psum_tensor(f"a_st{i}", [128, 1536], F32)) for i in range(2)]
    ps_y = pp[0][:, 0:128]
    ps_bz = pp[0][:, 128:256]
    ps_zr = pp[0][0:1, 0:TT]
    ps_zz = pp[0][0:1, 256:384]

    for dst_, src_ in ((cos_t, cosin.ap()[0]), (sin_t, cosin.ap()[1]),
                       (logp, logpos.ap()), (mwin, masks.ap()[0]),
                       (mdiag, masks.ap()[1]), (ident, identw.ap())):
        W_SM2 = emit("sync", (lambda dst_=dst_, src_=src_: lambda s: s.dma_start(
            out=dst_[:, :], in_=src_))(), inc="sm2", amt=16)

    for sw2 in range(2):
        pre = wm(("pp", 0)) + wm(("strip", 0)) + wm(("strip", 1))
        layernorm(sw2 * TT,
                  (lambda sw2=sw2: lambda c: xlnf[:, c * TLOC + sw2 * TT: c * TLOC + sw2 * TT + TT])(),
                  strip[0][0:1, 512:1024], strip[1][0:1, 512:1024],
                  strip[0][:, 0:TT], strip[1][:, 0:TT], pre=pre)
        d_ln2 = sch.n("dve")
        setwm(("pp", 0), "dve", d_ln2)
        for i in range(2):
            setwm(("strip", i), "dve", d_ln2)

    def proj_col(w_hbm, col_i, par):
        d = 0
        for h2 in range(4):
            qp = h2 % 2
            wv_ = emit("sync", (lambda w_hbm=w_hbm, col_i=col_i, h2=h2, qp=qp: lambda s: s.dma_start(
                out=afp[qp][:, :], in_=w_hbm.ap()[col_i][:, h2 * 512:(h2 + 1) * 512]))(),
                waits=wm(("afp", qp)), inc=f"sa{qp}", amt=16)
            d = emit("vector", (lambda par=par, h2=h2, qp=qp: lambda v: v.tensor_copy(
                acol[par][:, h2 * 512:(h2 + 1) * 512], afp[qp][:, :]))(),
                waits=[(f"sa{qp}", wv_)] + (wm(("acol", par)) if h2 == 0 else []), inc="dve")
            setwm(("afp", qp), "dve", d)
        return d

    def proj_mm_half(par, ppar, half, extra=(), ps=None, pskey=("pp", 0)):
        ps_ap = pp[0][:, :] if ps is None else ps
        for dc in range(DC):
            w = []
            if dc == 0:
                w = list(extra) + wm(pskey)
            emit("tensor", (lambda dc=dc, par=par, half=half, ps_ap=ps_ap: lambda t: t.matmul(
                ps_ap, acol[par][:, dc * 128:(dc + 1) * 128],
                xlnf[:, dc * TLOC + half * TT: dc * TLOC + half * TT + TT],
                start=(dc == 0), stop=(dc == DC - 1)))(), waits=w, inc="pe")
        return sch.n("pe")

    def qk_head(w_hbm, col_i, par, dest, qh, dest_guard=()):
        d_col = proj_col(w_hbm, col_i, par)
        dlast = 0
        for half in range(2):
            t0h, t1h = half * TT, (half + 1) * TT
            b64 = strip[1][0:64, t0h:t1h]
            extra = [("dve", d_col)] if half == 0 else []
            pe_p = proj_mm_half(par, half, half, extra=extra)
            if half == 1:
                setwm(("acol", par), "pe", pe_p)
            d_raw = emit("vector", (lambda half=half: lambda v: v.tensor_copy(
                praw[:, :], pp[0][:, :]))(),
                waits=[("pe", pe_p)] + wm("praw"), inc="dve")
            setwm(("pp", 0), "dve", d_raw)
            a_sq = emit("scalar", lambda s: s.activation(
                out=psq[:, :], in_=praw[:, :], func=AF.Square),
                waits=[("dve", d_raw)] + wm("psq"), inc="act")
            pz = emit("tensor", lambda t: t.matmul(
                ps_zr, onescol[:, :], psq[:, :], start=True, stop=True),
                waits=[("act", a_sq)] + wm("ps_zr"), inc="pe")
            setwm("psq", "pe", pz)
            aln = emit("scalar", lambda s: s.activation(
                out=rr1[:, :], in_=ps_zr, func=AF.Ln,
                bias=epsrms[:, :], scale=1.0 / DH),
                waits=[("pe", pz)] + wm("rr1"), inc="act")
            setwm("ps_zr", "act", aln)
            if qh is None:
                aex = emit("scalar", lambda s: s.activation(
                    out=rr2[:, :], in_=rr1[:, :], func=AF.Exp, scale=-0.5),
                    waits=[("act", aln)] + wm("rr2"), inc="act")
                setwm("rr1", "act", aex)
                row_w = [("act", aex)]
            else:
                aex = emit("scalar", lambda s: s.activation(
                    out=rr1[:, :], in_=rr1[:, :], func=AF.Exp, scale=-0.5),
                    waits=[("act", aln)], inc="act")
                d_row = emit("vector", (lambda half=half, qh=qh: lambda v: v.scalar_tensor_tensor(
                    rr2[:, :], rr1[:, :], scl[:, qh:qh + 1],
                    logp[:, half * TT:(half + 1) * TT], ALU.mult, ALU.mult))(),
                    waits=[("act", aex), ("sm2", W_SM2)] + wm("rr2"), inc="dve")
                setwm("rr1", "dve", d_row)
                row_w = [("dve", d_row)]
            pb = emit("tensor", (lambda half=half: lambda t: t.matmul(
                strip[1][0:64, half * TT:(half + 1) * TT], onesrowf[:, 0:64], rr2[:, :],
                start=True, stop=True))(),
                waits=row_w + wm(("strip", 1)), inc="pe")
            setwm("rr2", "pe", pb)
            setwm(("strip", 1), "pe", pb)
            wq2 = emit("sync", lambda s: s.dma_start(
                out=q2s[:, :], in_=praw[64:128, :]),
                waits=[("dve", d_raw)] + wm("q2s"), inc="sq2", amt=16)
            d = emit("vector", (lambda t0h=t0h, t1h=t1h: lambda v: v.tensor_mul(
                sc1[:, :], q2s[:, :], sin_t[:, t0h:t1h]))(),
                waits=[("sq2", wq2), ("sm2", W_SM2)] + wm("sc1"), inc="dve")
            d = emit("vector", (lambda t0h=t0h, t1h=t1h: lambda v: v.tensor_mul(
                sc2[:, :], praw[0:64, :], cos_t[:, t0h:t1h]))(),
                waits=wm("sc2"), inc="dve")
            d = emit("vector", lambda v: v.tensor_tensor(
                sc2[:, :], sc2[:, :], sc1[:, :], ALU.add),
                waits=[("dve", d)], inc="dve")
            dlow = emit("vector", (lambda dest=dest, t0h=t0h, t1h=t1h, b64=b64:
                                   lambda v: v.tensor_mul(
                dest[0:64, t0h:t1h], sc2[:, :], b64))(),
                waits=[("dve", d), ("pe", pb)] + list(dest_guard), inc="dve")
            d = emit("vector", (lambda t0h=t0h, t1h=t1h: lambda v: v.tensor_mul(
                sc1[:, :], praw[0:64, :], sin_t[:, t0h:t1h]))(),
                waits=[("dve", dlow)], inc="dve")
            d = emit("vector", (lambda t0h=t0h, t1h=t1h: lambda v: v.tensor_mul(
                sc2[:, :], q2s[:, :], cos_t[:, t0h:t1h]))(), inc="dve")
            d = emit("vector", lambda v: v.tensor_tensor(
                sc2[:, :], sc2[:, :], sc1[:, :], ALU.subtract),
                waits=[("dve", d)], inc="dve")
            d = emit("vector", (lambda t0h=t0h, t1h=t1h, b64=b64: lambda v: v.tensor_mul(
                y2s[:, :], sc2[:, :], b64))(),
                waits=[("dve", d)] + wm("y2s"), inc="dve")
            whigh = emit("sync", (lambda dest=dest, t0h=t0h, t1h=t1h: lambda s: s.dma_start(
                out=dest[64:128, t0h:t1h], in_=y2s[:, :]))(),
                waits=[("dve", d)], inc="ydma", amt=16)
            setwm("y2s", "ydma", whigh)
            setwm("q2s", "dve", d)
            setwm("sc1", "dve", d)
            setwm("sc2", "dve", d)
            last["praw"] = (("act", a_sq), ("dve", d), ("sq2", wq2))
            dlast = d
        addwm(("strip", 1), "dve", dlast)
        addwm(("pp", 0), "act", sch.n("act"))
        return dlast, whigh

    for kc in range(KVH):
        dest = K_all[:, kc * 2048 + 1024: kc * 2048 + 2048]
        dd, wh = qk_head(wkh, kc, kc % 2, dest, None)
        last[("kown", kc)] = (("dve", dd), ("ydma", wh))

    for dc in range(DC):
        par = dc % 2
        wv_w = emit("sync", (lambda dc=dc, par=par: lambda s: s.dma_start(
            out=afp[par][:, :], in_=wvh.ap()[dc * 128:(dc + 1) * 128, :]))(),
            waits=wm(("afp", par)), inc=f"sa{par}", amt=16)
        dv = emit("vector", (lambda dc=dc, par=par: lambda v: v.tensor_copy(
            wv_sb[:, dc * 512:(dc + 1) * 512], afp[par][:, :]))(),
            waits=[(f"sa{par}", wv_w)], inc="dve")
        setwm(("afp", par), "dve", dv)
    d_wv = sch.n("dve")
    for tc in range(8):
        vps = pp[0][:, :] if tc % 2 == 0 else strip[0][:, 0:TT]
        vkey = ("pp", 0) if tc % 2 == 0 else ("strip", 0)
        for dc in range(DC):
            w = ([("dve", d_wv)] + wm(vkey)) if dc == 0 else []
            emit("tensor", (lambda tc=tc, dc=dc, vps=vps: lambda t: t.matmul(
                vps,
                xlnf[:, dc * TLOC + tc * 128: dc * TLOC + (tc + 1) * 128],
                wv_sb[:, dc * 512:(dc + 1) * 512],
                start=(dc == 0), stop=(dc == DC - 1)))(), waits=w, inc="pe")
        pev = sch.n("pe")
        av = emit("scalar", (lambda tc=tc, vps=vps: lambda s: s.activation(
            out=V_all[:, (8 + tc) * 512:(9 + tc) * 512], in_=vps,
            func=AF.Copy))(), waits=[("pe", pev)], inc="act")
        setwm(vkey, "act", av)
        setwm(("vown", tc), "act", av)

    SGW = [0] * H
    for gh in range(H):
        gpar = gh % 2
        d_colg = proj_col(wgh, gh, gpar)
        for half in range(2):
            extra = [("dve", d_colg)] if half == 0 else []
            gsel = (gh * 2 + half) % 2
            gps = pp[0][:, :] if gsel == 0 else strip[0][:, 0:TT]
            gkey = ("pp", 0) if gsel == 0 else ("strip", 0)
            peg = proj_mm_half(gpar, half, half, extra=extra, ps=gps, pskey=gkey)
            ag = emit("scalar", (lambda half=half, gps=gps: lambda s: s.activation(
                out=sgtmp[:, :], in_=gps, func=AF.Silu))(),
                waits=[("pe", peg)] + wm("sgtmp"), inc="act")
            setwm(gkey, "act", ag)
            wsg = emit("sync", (lambda gh=gh, half=half: lambda s: s.dma_start(
                out=sgb.ap()[gh][:, half * TT:(half + 1) * TT],
                in_=sgtmp[:, :]))(), waits=[("act", ag)], inc="sgw", amt=16)
            setwm("sgtmp", "sgw", wsg)
        SGW[gh] = sch.n("sgw")
        setwm(("acol", gpar), "pe", sch.n("pe"))

    for kc in range(KVH):
        emit("gpsimd", (lambda kc=kc: lambda g: g.dma_start(
            out=kv_in.ap()[kc],
            in_=K_all[:, kc * 2048 + 1024: kc * 2048 + 2048]))(),
            waits=wm(("kown", kc)), inc="gdma", amt=16)
    for pr in range(4):
        emit("gpsimd", (lambda pr=pr: lambda g: g.dma_start(
            out=kv_in.ap()[4 + pr],
            in_=V_all[:, (8 + 2 * pr) * 512:(8 + 2 * pr + 2) * 512]))(),
            waits=wm(("vown", 2 * pr)) + wm(("vown", 2 * pr + 1)),
            inc="gdma", amt=16)
    gw = sch.n("gdma")
    c_kv = emit("gpsimd", lambda g: g.collective_compute(
        "AllGather", ALU.bypass, replica_groups=rg_pair,
        ins=[kv_in.ap()], outs=[kv_out.ap()]),
        waits=[("gdma", gw)], inc="cc")
    for kc in range(KVH):
        emit("gpsimd", (lambda kc=kc: lambda g: g.dma_start(
            out=K_all[:, kc * 2048: kc * 2048 + 1024], in_=kv_out.ap()[0, kc]))(),
            waits=[("cc", c_kv)], inc="gdma", amt=16)
    for pr in range(4):
        emit("gpsimd", (lambda pr=pr: lambda g: g.dma_start(
            out=V_all[:, 2 * pr * 512:(2 * pr + 2) * 512],
            in_=kv_out.ap()[0, 4 + pr]))(), inc="gdma", amt=16)
    W_CTXD = sch.n("gdma")
    D_CTX = 0
    for kc in range(KVH):
        D_CTX = emit("vector", (lambda kc=kc: lambda v: v.tensor_scalar(
            out=K_all[:, kc * 2048: kc * 2048 + 1024],
            in0=K_all[:, kc * 2048: kc * 2048 + 1024],
            scalar1=v01t[:, :], scalar2=None, op0=ALU.mult))(),
            waits=[("gdma", W_CTXD)], inc="dve")
    for vc in range(4):
        D_CTX = emit("vector", (lambda vc=vc: lambda v: v.tensor_scalar(
            out=V_all[:, vc * 1024:(vc + 1) * 1024],
            in0=V_all[:, vc * 1024:(vc + 1) * 1024],
            scalar1=v01t[:, :], scalar2=None, op0=ALU.mult))(),
            waits=[("gdma", W_CTXD)] if vc == 0 else [], inc="dve")
    W_CTX = sch.n("gdma")

    for h in range(H):
        kvh = h // 4
        par = h % 2
        dq, whq = qk_head(wqh, h, par, q_s[:, :], h, dest_guard=wm("q_s"))
        last["q_s_ready"] = (("dve", dq), ("ydma", whq))
        hpar = h % 2
        W_SG = emit("gpsimd", (lambda h=h, hpar=hpar: lambda g: g.dma_start(
            out=sg_h[:, hpar * TLOC:(hpar + 1) * TLOC], in_=sgb.ap()[h]))(),
            waits=[("sgw", SGW[h])] + wm(("sgh", hpar)), inc=f"sgr{hpar}", amt=16)

        S = [dict() for _ in range(8)]
        for k in range(12):
            sA = k <= 7
            sB = 2 <= k <= 9
            sC = 3 <= k <= 10
            sD = k >= 4
            qa, qb, qc, qd = k, k - 2, k - 3, k - 4
            if sD:
                zi = qd % 3
                bz = strip[qd % 2][:, 1280:1408]
                dzn = emit("vector", (lambda zi=zi, bz=bz: lambda v: v.scalar_tensor_tensor(
                    z_t[:, zi * 128:(zi + 1) * 128], z_t[:, zi * 128:(zi + 1) * 128],
                    gn[:, :], bz, ALU.mult, ALU.mult))(),
                    waits=[("pe", S[qd]["pb2"])], inc="dve")
                setwm(("bz", qd % 2), "dve", dzn)
                dxa = emit("vector", (lambda h=h, qd=qd, zi=zi: lambda v: v.tensor_tensor(
                    x[:, h * TLOC + qd * 128: h * TLOC + (qd + 1) * 128],
                    x[:, h * TLOC + qd * 128: h * TLOC + (qd + 1) * 128],
                    z_t[:, zi * 128:(zi + 1) * 128], ALU.add))(),
                    waits=[("dve", dzn)], inc="dve")
                setwm(("zt", zi), "dve", dxa)
            if sB:
                pj = qb % 3
                for j in range(8):
                    w = []
                    if j == 0:
                        w = wm("ptp") + [("act", S[qb]["a2"])]
                    emit("tensor", (lambda j=j, pj=pj: lambda t: t.transpose(
                        ptp[:, j * 128:(j + 1) * 128],
                        p_sb[:, pj * 1152 + j * 128: pj * 1152 + (j + 1) * 128],
                        ident[:, :]))(), waits=w, inc="pe")
                S[qb]["pe8a"] = sch.n("pe")
                ev1 = emit("vector", (lambda: lambda v: v.tensor_copy(
                    pT[:, 0:1024], ptp[:, :]))(),
                    waits=[("pe", S[qb]["pe8a"])] + wm("pT"), inc="dve")
                S[qb]["ev1"] = ev1
            if sC:
                zi = qc % 3
                sp = qc % 2
                zz = strip[sp][0:1, 1408:1536]
                w = [("act", S[qc]["az"])] + wm(("zq", zi))
                if qc + 2 <= 7:
                    w.append(("act", S[qc + 2]["a2"]))
                pz = emit("tensor", (lambda zi=zi, zz=zz: lambda t: t.matmul(
                    zz, onescol[:, :], zq_t[:, zi * 128:(zi + 1) * 128],
                    start=True, stop=True))(), waits=w, inc="pe")
                setwm(("zq", zi), "pe", pz)
                S[qc]["pz"] = pz
            if sB:
                pj = qb % 3
                pe8 = emit("tensor", (lambda pj=pj: lambda t: t.transpose(
                    ptp[:, 0:128],
                    p_sb[:, pj * 1152 + 8 * 128: pj * 1152 + 9 * 128],
                    ident[:, :]))(), waits=[("dve", S[qb]["ev1"])], inc="pe")
                setwm(("p_sb", pj), "pe", pe8)
                S[qb]["pe8"] = pe8
                ev = emit("vector", (lambda: lambda v: v.tensor_copy(
                    pT[:, 1024:1152], ptp[:, 0:128]))(),
                    waits=[("pe", pe8)], inc="dve")
                setwm("ptp", "dve", ev)
                S[qb]["ev"] = ev
                sp = qb % 2
                kvh_ = kvh
                ys = strip[sp][:, 1152:1280]
                for j in range(9):
                    kt = qb + j
                    w = []
                    if j == 0:
                        w = [("dve", S[qb]["ev1"])] + wm(("ys", sp))
                        if qb == 0 and h == 0:
                            w.append(("dve", D_CTX))
                    if j == 8:
                        w = [("dve", ev)]
                    emit("tensor", (lambda j=j, kt=kt, kvh_=kvh_, ys=ys:
                                    lambda t: t.matmul(
                        ys,
                        V_all[:, kt * 512 + kvh_ * 128: kt * 512 + kvh_ * 128 + 128],
                        pT[:, j * 128:(j + 1) * 128],
                        start=(j == 0), stop=(j == 8)))(), waits=w, inc="pe")
                pe_y = sch.n("pe")
                S[qb]["pe_y"] = pe_y
                setwm("pT", "pe", pe_y)
            if sA:
                sp = qa % 2
                st = strip[sp]
                qtile = q_s[:, qa * 128:(qa + 1) * 128]
                kbase = kvh * 2048 + qa * 128
                for pc, (off, wdt) in enumerate(((0, 512), (512, 512), (1024, 128))):
                    w = []
                    if pc == 0:
                        w = wm(("strip", sp))
                        if qa == 0:
                            w = w + (wm("q_s_ready")
                                     + [("gdma", W_CTX), ("dve", D_CTX),
                                        ("sm2", W_SM2)])
                    emit("tensor", (lambda st=st, off=off, wdt=wdt, qtile=qtile,
                                    kbase=kbase, pc=pc: lambda t: t.matmul(
                        st[:, off:off + wdt], qtile,
                        K_all[:, kbase + off: kbase + off + wdt],
                        start=True, stop=(pc == 1)))(), waits=w, inc="pe")
                    if pc == 0:
                        emit("tensor", (lambda st=st: lambda t: t.matmul(
                            st[:, 0:128], ident[:, :], mwin[:, :],
                            start=False, stop=True))(), inc="pe")
                    if pc == 2:
                        emit("tensor", (lambda st=st: lambda t: t.matmul(
                            st[:, 1024:1152], ident[:, :], mdiag[:, :],
                            start=False, stop=True))(), inc="pe")
                S[qa]["pe_qk"] = sch.n("pe")
            if sC:
                sp = qc % 2
                zz = strip[sp][0:1, 1408:1536]
                au = emit("scalar", (lambda sp=sp, zz=zz: lambda s: s.activation(
                    out=rr1[:, sp * 128:(sp + 1) * 128], in_=zz, func=AF.Ln,
                    bias=epsrms[:, :], scale=1.0 / DH))(),
                    waits=[("pe", S[qc]["pz"])], inc="act")
                setwm(("zz", sp), "act", au)
                ar2 = emit("scalar", (lambda sp=sp: lambda s: s.activation(
                    out=rr2[:, sp * 128:(sp + 1) * 128],
                    in_=rr1[:, sp * 128:(sp + 1) * 128], func=AF.Exp, scale=-0.5))(),
                    waits=[("act", au)], inc="act")
                pb2 = emit("tensor", (lambda sp=sp: lambda t: t.matmul(
                    strip[sp][:, 1280:1408], onesrowf[:, :],
                    rr2[:, sp * 128:(sp + 1) * 128],
                    start=True, stop=True))(),
                    waits=[("act", ar2)] + wm(("bz", sp)), inc="pe")
                S[qc]["pb2"] = pb2
            if sB:
                sp = qb % 2
                zi = qb % 3
                ys = strip[sp][:, 1152:1280]
                w = [("pe", S[qb]["pe_y"])] + wm(("zt", zi))
                if qb == 0:
                    w.append((f"sgr{hpar}", W_SG))
                dz = emit("vector", (lambda qb=qb, zi=zi, ys=ys, hpar=hpar:
                                     lambda v: v.tensor_mul(
                    z_t[:, zi * 128:(zi + 1) * 128], ys,
                    sg_h[:, hpar * TLOC + qb * 128: hpar * TLOC + (qb + 1) * 128]))(),
                    waits=w, inc="dve")
                setwm(("ys", sp), "dve", dz)
                az = emit("scalar", (lambda zi=zi: lambda s: s.activation(
                    out=zq_t[:, zi * 128:(zi + 1) * 128],
                    in_=z_t[:, zi * 128:(zi + 1) * 128], func=AF.Square))(),
                    waits=[("dve", dz)], inc="act")
                S[qb]["az"] = az
            if sA:
                sp = qa % 2
                pj = qa % 3
                st = strip[sp]
                dmx = emit("vector", (lambda st=st, sp=sp: lambda v: v.reduce_max(
                    den_t[:, sp * 4: sp * 4 + 1], st[:, 0:1152],
                    AXX, negate=True))(),
                    waits=[("pe", S[qa]["pe_qk"])] + wm(("dent", sp)), inc="dve")
                a2 = emit("scalar", (lambda st=st, sp=sp, pj=pj: lambda s: s.activation(
                    out=p_sb[:, pj * 1152: pj * 1152 + 1152], in_=st[:, 0:1152],
                    func=AF.Exp, bias=den_t[:, sp * 4: sp * 4 + 1], scale=1.0))(),
                    waits=[("dve", dmx)] + wm(("p_sb", pj)), inc="act")
                setwm(("strip", sp), "act", a2)
                setwm(("dent", sp), "act", a2)
                S[qa]["a2"] = a2

        setwm("q_s", "pe", sch.n("pe"))
        setwm(("sgh", hpar), "dve", sch.n("dve"))

    astk.close()
    sch.barrier()
    clear_ffn_keys()

    with contextlib.ExitStack() as fstk:
        ft2 = ffn_alloc(fstk, "f2", False)
        ffn_sweep(ft2, 0, quant=False)
        ffn_sweep(ft2, 1, quant=False)

    for c in range(DC):
        w = wm(("xout", 0, c)) + wm(("xout", TT, c))
        emit("gpsimd", (lambda c=c: lambda g: g.dma_start(
            out=out_fm.ap()[c * 128:(c + 1) * 128, :],
            in_=x[:, c * TLOC:(c + 1) * TLOC]))(), waits=w, inc="gdma", amt=16)
    sch.ops["gpsimd"].append((None, (("gdma", sch.n("gdma")),), None, 0, None))

    with nc.Block() as block:
        @block.sync
        def _(e):
            sch.replay("sync", e, sems)

        @block.tensor
        def _(e):
            sch.replay("tensor", e, sems)

        @block.vector
        def _(e):
            sch.replay("vector", e, sems)

        @block.scalar
        def _(e):
            sch.replay("scalar", e, sems)

        @block.gpsimd
        def _(e):
            sch.replay("gpsimd", e, sems)

    ctx.close()
    return nc


def _stage(inputs, n_cores=8):
    x = np.asarray(inputs["L0_attn_hidden_states"], dtype=np.float32)
    wq = np.asarray(inputs["L0_attn_Wq"], np.float32)
    wk = np.asarray(inputs["L0_attn_Wk"], np.float32)
    wv = np.asarray(inputs["L0_attn_Wv"], np.float32)
    wg = np.asarray(inputs["L0_attn_Wg"], np.float32)
    sscal = np.asarray(inputs["L0_attn_softmax_scaler"], np.float32)
    gnw = np.asarray(inputs["L0_attn_gn_weight"], np.float32)
    wu = np.asarray(inputs["L0_ffn_W_up"], np.float32)
    wgt = np.asarray(inputs["L0_ffn_W_gate"], np.float32)
    wd = np.asarray(inputs["L0_ffn_W_down"], np.float32)

    def colmajor_up(w):
        return np.ascontiguousarray(
            w.reshape(EC, 128, DC, 128).transpose(0, 3, 2, 1)).reshape(EC, 128, 2048)

    tu_t = colmajor_up(wu)
    tg_t = colmajor_up(wgt)
    td_t = np.ascontiguousarray(
        wd.reshape(DC, 128, EC, 128).transpose(0, 3, 2, 1)).reshape(DC, 128, 8192)

    def headcols(w, nh):
        return np.ascontiguousarray(
            w.reshape(DC, 128, nh, 128).transpose(2, 1, 0, 3)).reshape(nh, 128, 2048)

    wqh = headcols(wq, H)
    wkh = headcols(wk, KVH)
    wgh = headcols(wg, H)

    shards = [w.reshape(8, 128, 16384) for w in (wu, wgt, wd)]
    inv_freq = (1.0 / (10000.0 ** (np.arange(0, DH, 2, dtype=np.float32) / DH))).astype(np.float32)

    ii = np.arange(128)[:, None]
    jj = np.arange(128)[None, :]
    masks = np.stack([
        np.where(jj >= ii, 0.0, NEGBIG).astype(np.float32),
        np.where(jj <= ii, 0.0, NEGBIG).astype(np.float32)]).astype(bf16)

    in_maps = []
    for c in range(n_cores):
        b, half = c // 2, c % 2
        t0 = half * TLOC
        pos = np.arange(t0, t0 + TLOC, dtype=np.float32)
        freqs = np.outer(pos, inv_freq).astype(np.float32)
        cosin = np.ascontiguousarray(
            np.stack([np.cos(freqs).T, np.sin(freqs).T])).astype(bf16)
        im = {
            "x_fm": np.ascontiguousarray(x[b, t0:t0 + TLOC, :].T),
            "wqh": wqh, "wkh": wkh, "wvh": wv, "wgh": wgh,
            "tu_t": tu_t, "tg_t": tg_t, "td_t": td_t,
            "shard": np.ascontiguousarray(
                np.stack([shards[0][c % 8], shards[1][c % 8], shards[2][c % 8]])),
            "cosin": cosin,
            "logpos": np.log(pos + 1.0).astype(bf16)[None, :],
            "scaler": sscal[None, :].astype(np.float32),
            "gnw": gnw[:, None].astype(np.float32),
            "v01": np.full((128, 1), 1.0 if half == 1 else 0.0, np.float32),
            "masks": masks, "identw": np.eye(128, dtype=bf16),
        }
        in_maps.append(im)
    return in_maps


def kernel(**inputs):
    in_maps = _stage(inputs, 8)
    nc = build_nc(8)
    res = run_bass_kernel_spmd(nc, in_maps, core_ids=list(range(8)))
    out = np.empty((4, 2048, 2048), dtype=np.float32)
    for c in range(8):
        b, half = c // 2, c % 2
        out[b, half * TLOC:(half + 1) * TLOC, :] = np.asarray(res.results[c]["out_fm"]).T
    return out
